# version 01 (final)
# baseline (speedup 1.0000x reference)
import contextlib
import numpy as np
import concourse.bass as bass
import concourse.mybir as mybir
from concourse.bass_utils import run_bass_kernel_spmd

F32 = mybir.dt.float32
BF16 = mybir.dt.bfloat16
U32 = mybir.dt.uint32
AF = mybir.ActivationFunctionType
ALU = mybir.AluOpType
AX = mybir.AxisListType

ENGINES = ("tensor", "vector", "scalar", "gpsimd", "sync")
SKIP = set()


class _Op:
    __slots__ = ("eng", "emit", "deps", "need_inc", "inc_idx", "dma_sem", "dma_idx",
                 "waits", "tag", "cre")

    def __init__(self, eng, emit, tag, cre):
        self.eng = eng
        self.emit = emit
        self.deps = []
        self.need_inc = False
        self.inc_idx = 0
        self.dma_sem = None
        self.dma_idx = 0
        self.waits = None
        self.tag = tag
        self.cre = cre


def _bbox(ap):
    h = ap.tensor
    tn = type(h).__name__
    if tn == "SBTensorHandle" or tn == "PSumTensorHandle":
        F = int(h.shape[1]) if len(h.shape) > 1 else 1
        if tn == "PSumTensorHandle":
            return (h.name, True, 0, 127, 0, max(F - 1, 0))
        off = ap.offset
        p0 = off // F
        f0 = off % F
        pe = 0
        fe = 0
        for step, count in ap.ap:
            if count <= 1 or step == 0:
                continue
            s = abs(step)
            if s % F == 0:
                pe += (s // F) * (count - 1)
            else:
                assert s < F, f"ambiguous AP step {step} for tensor {h.name} F={F}"
                fe += s * (count - 1)
        return (h.name, False, p0, p0 + pe, f0, f0 + fe)
    else:
        off = ap.offset
        ext = 0
        for step, count in ap.ap:
            if count > 1:
                ext += abs(step) * (count - 1)
        return (h.name, False, 0, 0, off, off + ext)


def _overlap(a, b):
    return not (a[3] < b[2] or b[3] < a[2] or a[5] < b[4] or b[5] < a[4])


def _covers(a, b):
    return a[2] <= b[2] and a[3] >= b[3] and a[4] <= b[4] and a[5] >= b[5]


class Prog:
    def __init__(self, nc, es):
        self.nc = nc
        self.es = es
        self.eng_ops = {e: [] for e in ENGINES}
        self.acc = {}
        self.cre = 0
        self.sems = {}
        for e in ENGINES:
            self.sems["e_" + e] = es.enter_context(nc.semaphore("sem_" + e))
        self.dma_counts = {}
        self.n_edges = 0

    def dma_sem(self, stream, parity):
        name = f"d_{stream}_{parity & 1}"
        if name not in self.sems:
            self.sems[name] = self.es.enter_context(self.nc.semaphore(name))
            self.dma_counts[name] = 0
        return name

    def _log_and_deps(self, op, reads, writes):
        deps = []
        for r in reads:
            bb = _bbox(r)
            lst = self.acc.get(bb[0])
            if lst:
                for i in range(len(lst) - 1, -1, -1):
                    is_w, obb, oop = lst[i]
                    if not _overlap(bb, obb):
                        continue
                    if is_w:
                        if oop is not op:
                            deps.append(oop)
                        if _covers(obb, bb):
                            break
        for w in writes:
            bb = _bbox(w)
            lst = self.acc.get(bb[0])
            if lst:
                for i in range(len(lst) - 1, -1, -1):
                    is_w, obb, oop = lst[i]
                    if not _overlap(bb, obb):
                        continue
                    if oop is not op:
                        deps.append(oop)
                    if is_w and _covers(obb, bb):
                        break
        for r in reads:
            bb = _bbox(r)
            self.acc.setdefault(bb[0], []).append((False, bb, op))
        for w in writes:
            bb = _bbox(w)
            self.acc.setdefault(bb[0], []).append((True, bb, op))
        seen = set()
        out = []
        for d in deps:
            if d is op:
                continue
            if d.eng == "tensor" and op.eng == "tensor" and d.dma_sem is None:
                continue
            if id(d) in seen:
                continue
            seen.add(id(d))
            if d.dma_sem is not None:
                out.append((d, self.dma_counts[d.dma_sem]))
            else:
                out.append((d, 0))
        op.deps = out
        self.n_edges += len(out)

    def add(self, eng, emit, reads=(), writes=(), tag=""):
        op = _Op(eng, emit, tag, self.cre)
        self.cre += 1
        self._log_and_deps(op, reads, writes)
        self.eng_ops[eng].append(op)
        return op

    def dma_like(self, eng, stream, parity, emit, reads, writes, tag=""):
        op = _Op(eng, emit, tag, self.cre)
        self.cre += 1
        sem = self.dma_sem(stream, parity)
        self._log_and_deps(op, reads=reads, writes=writes)
        op.dma_sem = sem
        self.dma_counts[sem] += 1
        op.dma_idx = self.dma_counts[sem]
        self.eng_ops[eng].append(op)
        return op

    def dma(self, eng, stream, parity, out, in_, tag="", unique=False, **kw):
        if unique:
            n = getattr(self, "_uniq", 0)
            self._uniq = n + 1
            stream = f"u{n}"
            parity = 0
        op = _Op(eng, None, tag, self.cre)
        self.cre += 1
        sem = self.dma_sem(stream, parity)
        op.emit = lambda e, out=out, in_=in_, kw=kw: e.dma_start(out=out, in_=in_, **kw)
        self._log_and_deps(op, reads=[in_], writes=[out])
        op.dma_sem = sem
        self.dma_counts[sem] += 1
        op.dma_idx = self.dma_counts[sem]
        self.eng_ops[eng].append(op)
        return op

    def finalize(self, block, final_waits=()):
        for e in ENGINES:
            for op in self.eng_ops[e]:
                for d, _snap in op.deps:
                    if d.dma_sem is None:
                        d.need_inc = True
        for e in ENGINES:
            n = 0
            for op in self.eng_ops[e]:
                if op.dma_sem is None and op.need_inc:
                    n += 1
                    op.inc_idx = n
        for e in ENGINES:
            for op in self.eng_ops[e]:
                w = {}
                for d, snap in op.deps:
                    if d.dma_sem is not None:
                        sem, val = d.dma_sem, 16 * snap
                    else:
                        sem, val = "e_" + d.eng, d.inc_idx
                    if w.get(sem, 0) < val:
                        w[sem] = val
                op.waits = sorted(w.items())
        for e in ENGINES:
            ops = self.eng_ops[e]
            if not ops and not (e == "sync" and final_waits):
                continue

            def body(eng, e=e, ops=ops):
                waited = {}
                for op in ops:
                    for sem, val in op.waits:
                        if waited.get(sem, 0) < val:
                            eng.wait_ge(self.sems[sem], val)
                            waited[sem] = val
                    inst = op.emit(eng)
                    if op.dma_sem is not None:
                        inst.then_inc(self.sems[op.dma_sem], 16)
                    elif op.need_inc:
                        inst.then_inc(self.sems["e_" + e], 1)
                if e == "sync":
                    for sem, val in final_waits:
                        if waited.get(sem, 0) < val:
                            eng.wait_ge(self.sems[sem], val)
                            waited[sem] = val
            getattr(block, e)(body)

    def final_dma_waits(self):
        return [(name, 16 * cnt) for name, cnt in self.dma_counts.items()]



class Ctx:
    def __init__(self, nc, prog):
        self.nc = nc
        self.prog = prog
        self.psum = []
        self.misc = {}


def emit_ffn_expertsum(C, *, E, T, D, H, HB, act,
                       xn_chunk,
                       xres_chunk,
                       w1_dram,
                       w2_dram,
                       b1_sb,
                       b2_sb,
                       wrow_dram,
                       wr_stage,
                       h_sb,
                       w1buf,
                       w2buf,
                       wb_sb,
                       tmp_sb,
                       ones_f32,
                       ps_up,
                       ps_y,
                       ps_b,
                       tag=""):
    p = C.prog
    ND = D // 128
    NO = D // 128
    NT = T // 512
    NHB = H // HB
    NHT = HB // 128
    up_i = 0
    dn_i = 0
    for e in range(E):
        p.dma("gpsimd", "wr" + tag, 0, wr_stage, wrow_dram(e), tag=f"{tag}wrdma")
        wr = wr_stage
        for tc in range(NT):
            p.add("tensor",
                  lambda t, wr=wr, tc=tc: t.matmul(ps_b[0][:, 0:512], ones_f32,
                                                   wr[:, tc*512:(tc+1)*512],
                                                   start=True, stop=True),
                  reads=[ones_f32, wr], writes=[ps_b[0]], tag=f"{tag}wbmm{e}")
            p.add("vector",
                  lambda v, tc=tc: v.tensor_copy(wb_sb[:, tc*512:(tc+1)*512], ps_b[0][:, 0:512]),
                  reads=[ps_b[0]], writes=[wb_sb[:, tc*512:(tc+1)*512]], tag=f"{tag}wbcp{e}")
        for blk in range(NHB):
            for win in range(HB // 256):
                pp = (e * NHB * (HB // 256) + blk * (HB // 256) + win) & 1
                h0 = blk * HB + win * 256
                src = w1_dram(e)[:, h0:h0+256].rearrange("(c p) o -> p c o", p=128)
                p.dma("gpsimd", "w1", pp,
                      w1buf[pp][:, 0:ND*256].rearrange("p (c o) -> p c o", o=256),
                      src, tag=f"{tag}w1dma")
                for htl in range(2):
                    ht = blk * NHT + win * 2 + htl
                    b1col = b1_sb[:, e*(H//128) + ht : e*(H//128) + ht + 1]
                    hl = (win * 2 + htl) * T
                    for tc in range(NT):
                        bank = ps_up[up_i & 1]
                        up_i += 1
                        for d in range(ND):
                            p.add("tensor",
                                  lambda t, d=d, htl=htl, tc=tc, pp=pp, bank=bank:
                                  t.matmul(bank[:, 0:512],
                                           w1buf[pp][:, d*256 + htl*128 : d*256 + htl*128 + 128],
                                           xn_chunk(d)[:, tc*512:(tc+1)*512],
                                           start=(d == 0), stop=(d == ND-1)),
                                  reads=[w1buf[pp][:, d*256 + htl*128 : d*256 + htl*128 + 128],
                                         xn_chunk(d)[:, tc*512:(tc+1)*512]],
                                  writes=[bank], tag=f"{tag}up{e}.{blk}.{tc}")
                        p.add("scalar",
                              lambda s, hl=hl, tc=tc, b1col=b1col, bank=bank:
                              s.activation(h_sb[:, hl + tc*512 : hl + tc*512 + 512],
                                           bank[:, 0:512], act, bias=b1col, scale=1.0),
                              reads=[bank, b1col],
                              writes=[h_sb[:, hl + tc*512 : hl + tc*512 + 512]],
                              tag=f"{tag}gelu{e}.{blk}.{tc}")
            for o in range(NO):
                pp2 = (e * NHB * NO + blk * NO + o) & 1
                src = w2_dram(e)[blk*HB:(blk+1)*HB, o*128:(o+1)*128]
                dst = w2buf[pp2][:, 0:NHT*128].rearrange("p (c o) -> p c o", o=128)
                srcr = src.rearrange("(c p) o -> p c o", p=128)
                p.dma("gpsimd", "w2", pp2, dst, srcr, tag=f"{tag}w2dma")
                b2col = b2_sb[:, e*NO + o : e*NO + o + 1]
                for tc in range(NT):
                    bank = ps_y[dn_i & 1]
                    for hc in range(NHT):
                        p.add("tensor",
                              lambda t, hc=hc, o=o, tc=tc, pp2=pp2, bank=bank:
                              t.matmul(bank[:, 0:512],
                                       w2buf[pp2][:, hc*128:(hc+1)*128],
                                       h_sb[:, hc*T + tc*512 : hc*T + tc*512 + 512],
                                       start=(hc == 0), stop=(hc == NHT-1)),
                              reads=[w2buf[pp2][:, hc*128:(hc+1)*128],
                                     h_sb[:, hc*T + tc*512 : hc*T + tc*512 + 512]],
                              writes=[bank], tag=f"{tag}dn{e}.{blk}.{o}.{tc}")
                    xs = xres_chunk(o)[:, tc*512:(tc+1)*512]
                    wbs = wb_sb[:, tc*512:(tc+1)*512]
                    if blk == 0:
                        p.add("vector",
                              lambda v, bank=bank, b2col=b2col, wbs=wbs:
                              v.scalar_tensor_tensor(tmp_sb[:, 0:512], bank[:, 0:512],
                                                     b2col, wbs, ALU.add, ALU.mult),
                              reads=[bank, b2col, wbs], writes=[tmp_sb], tag=f"{tag}ev1")
                    else:
                        p.add("vector",
                              lambda v, bank=bank, wbs=wbs:
                              v.tensor_tensor(tmp_sb[:, 0:512], bank[:, 0:512], wbs, ALU.mult),
                              reads=[bank, wbs], writes=[tmp_sb], tag=f"{tag}ev1")
                    p.add("vector",
                          lambda v, xs=xs: v.tensor_tensor(xs, xs, tmp_sb[:, 0:512], ALU.add),
                          reads=[xs, tmp_sb], writes=[xs], tag=f"{tag}ev2")
                    dn_i += 1


def emit_layernorm(C, *, T, D, xres_chunk, xn_chunk, stats, ps_a, ps_b, sq_sb, tag=""):
    p = C.prog
    ND = D // 128
    NT = T // 512
    onec = C.ones_col
    for tc in range(NT):
        for d in range(ND):
            xs = xres_chunk(d)[:, tc*512:(tc+1)*512]
            p.add("tensor", lambda t, xs=xs, d=d: t.matmul(ps_a[0:1, 0:512], onec, xs,
                                                           start=(d == 0), stop=(d == ND-1)),
                  reads=[onec, xs], writes=[ps_a], tag=f"{tag}lns1")
            p.add("scalar", lambda s, xs=xs: s.activation(sq_sb[:, 0:512], xs, AF.Square),
                  reads=[xs], writes=[sq_sb], tag=f"{tag}lnsq")
            p.add("tensor", lambda t, d=d: t.matmul(ps_b[0:1, 0:512], onec, sq_sb[:, 0:512],
                                                    start=(d == 0), stop=(d == ND-1)),
                  reads=[onec, sq_sb], writes=[ps_b], tag=f"{tag}lns2")
        s1 = stats[0:1, 0:512]
        s2 = stats[0:1, 512:1024]
        rst = stats[0:1, 1024:1536]
        p.add("vector", lambda v, s1=s1: v.tensor_scalar_mul(s1, ps_a[0:1, 0:512], 1.0 / D),
              reads=[ps_a], writes=[s1], tag=f"{tag}lnm")
        p.add("vector", lambda v, s2=s2: v.tensor_scalar(s2, ps_b[0:1, 0:512], 1.0 / D, 1e-5,
                                                         ALU.mult, ALU.add),
              reads=[ps_b], writes=[s2], tag=f"{tag}lnv")
        p.add("scalar", lambda s, s1=s1, rst=rst: s.activation(rst, s1, AF.Square),
              reads=[s1], writes=[rst], tag=f"{tag}lnm2")
        p.add("vector", lambda v, s2=s2, rst=rst: v.tensor_tensor(s2, s2, rst, ALU.subtract),
              reads=[s2, rst], writes=[s2], tag=f"{tag}lnv2")
        p.add("scalar", lambda s, s2=s2: s.activation(s2, s2, AF.Sqrt),
              reads=[s2], writes=[s2], tag=f"{tag}lnsd")
        p.add("vector", lambda v, s2=s2, rst=rst: v.reciprocal(rst, s2),
              reads=[s2], writes=[rst], tag=f"{tag}lnr")
        p.add("tensor", lambda t, s1=s1: t.matmul(ps_a[:, 0:512], C.ones_k1, s1,
                                                  start=True, stop=True),
              reads=[C.ones_k1, s1], writes=[ps_a], tag=f"{tag}lnbm")
        p.add("tensor", lambda t, rst=rst: t.matmul(ps_b[:, 0:512], C.ones_k1, rst,
                                                    start=True, stop=True),
              reads=[C.ones_k1, rst], writes=[ps_b], tag=f"{tag}lnbr")
        for d in range(ND):
            xs = xres_chunk(d)[:, tc*512:(tc+1)*512]
            xo = xn_chunk(d)[:, tc*512:(tc+1)*512]
            p.add("vector", lambda v, xs=xs: v.tensor_tensor(sq_sb[:, 0:512], xs,
                                                             ps_a[:, 0:512], ALU.subtract),
                  reads=[xs, ps_a], writes=[sq_sb], tag=f"{tag}lnc")
            p.add("vector", lambda v, xo=xo: v.tensor_tensor(xo, sq_sb[:, 0:512],
                                                             ps_b[:, 0:512], ALU.mult),
                  reads=[sq_sb, ps_b], writes=[xo], tag=f"{tag}lnx")


def emit_ffn_l2(C, *, T, D, xres_chunk, xn_chunk, w_dram, alpha_dram, bias_sb,
                wstage, wtT, ps_a, ps_b, sc1, tag=""):
    p = C.prog
    ND = D // 128
    NT = T // 512
    onec = C.ones_col
    acc = C.misc["l2acc"]
    for c in range(ND):
        slot = wstage[:, (c & 1) * 1024:(c & 1) * 1024 + 1024]
        p.dma("gpsimd", "l2w", c & 1, slot, w_dram[c*128:(c+1)*128, :], tag=f"{tag}wda")
        p.add("vector", lambda v, slot=slot: v.tensor_reduce(sc1[0:1, 0:1] if False else C.misc["l2red"][:, 0:1],
                                                             slot, AX.X, ALU.add,
                                                             apply_absolute_value=True),
              reads=[slot], writes=[C.misc["l2red"]], tag=f"{tag}red")
        if c == 0:
            p.add("vector", lambda v: v.tensor_copy(acc[:, 0:1], C.misc["l2red"][:, 0:1]),
                  reads=[C.misc["l2red"]], writes=[acc], tag=f"{tag}acc")
        else:
            p.add("vector", lambda v: v.tensor_tensor(acc[:, 0:1], acc[:, 0:1],
                                                      C.misc["l2red"][:, 0:1], ALU.add),
                  reads=[acc, C.misc["l2red"]], writes=[acc], tag=f"{tag}acc")
    p.add("tensor", lambda t: t.matmul(ps_a[0:1, 0:1], acc[:, 0:1], onec,
                                       start=True, stop=True),
          reads=[acc, onec], writes=[ps_a], tag=f"{tag}xs")
    p.add("vector", lambda v: v.tensor_scalar_mul(sc1[0:1, 0:1], ps_a[0:1, 0:1],
                                                  0.7 / (D * D)),
          reads=[ps_a], writes=[sc1[0:1, 0:1]], tag=f"{tag}dl")
    p.add("vector", lambda v: v.tensor_scalar_mul(sc1[0:1, 1:2], sc1[0:1, 0:1], -1.0),
          reads=[sc1[0:1, 0:1]], writes=[sc1[0:1, 1:2]], tag=f"{tag}dl2")
    p.add("tensor", lambda t: t.matmul(ps_b[:, 0:2], C.ones_k1, sc1[0:1, 0:2],
                                       start=True, stop=True),
          reads=[C.ones_k1, sc1[0:1, 0:2]], writes=[ps_b], tag=f"{tag}dbx")
    dcol = C.misc["l2dcol"]
    p.add("vector", lambda v: v.tensor_copy(dcol[:, 0:2], ps_b[:, 0:2]),
          reads=[ps_b], writes=[dcol], tag=f"{tag}dcp")
    acol = C.misc["l2acol"]
    p.dma("gpsimd", "l2a", 0, sc1[0:1, 2:3], alpha_dram, tag=f"{tag}al")
    p.add("tensor", lambda t: t.matmul(ps_a[:, 0:1], C.ones_k1, sc1[0:1, 2:3],
                                       start=True, stop=True),
          reads=[C.ones_k1, sc1[0:1, 2:3]], writes=[ps_a], tag=f"{tag}alb")
    p.add("vector", lambda v: v.tensor_copy(acol[:, 0:1], ps_a[:, 0:1]),
          reads=[ps_a], writes=[acol], tag=f"{tag}alc")
    tern = C.misc["l2tern"]
    gtb = C.misc["l2gt"]
    for c in range(ND):
        slot = wstage[:, (c & 1) * 1024:(c & 1) * 1024 + 1024]
        p.dma("gpsimd", "l2w2", c & 1, slot, w_dram[c*128:(c+1)*128, :], tag=f"{tag}wdb")
        p.add("vector", lambda v, slot=slot: v.tensor_scalar(gtb[:], slot,
                                                             dcol[:, 0:1], None, ALU.is_gt),
              reads=[slot, dcol], writes=[gtb], tag=f"{tag}gt")
        p.add("vector", lambda v, slot=slot: v.tensor_scalar(slot, slot,
                                                             dcol[:, 1:2], None, ALU.is_lt),
              reads=[slot, dcol], writes=[slot], tag=f"{tag}lt")
        p.add("vector", lambda v, slot=slot: v.tensor_tensor(tern[:], gtb[:], slot,
                                                             ALU.subtract),
              reads=[gtb, slot], writes=[tern], tag=f"{tag}tr")
        for d in range(ND):
            p.add("tensor", lambda t, d=d: t.transpose(C.ps_t[:, 0:128],
                                                       tern[:, d*128:(d+1)*128],
                                                       C.ident_bf),
                  reads=[tern[:, d*128:(d+1)*128], C.ident_bf], writes=[C.ps_t],
                  tag=f"{tag}tp")
            p.add("vector", lambda v, d=d, c=c: v.tensor_copy(
                      wtT[:, d*1024 + c*128: d*1024 + c*128 + 128], C.ps_t[:, 0:128]),
                  reads=[C.ps_t], writes=[wtT[:, d*1024 + c*128: d*1024 + c*128 + 128]],
                  tag=f"{tag}te")
    for o in range(ND):
        for tc in range(NT):
            bank = ps_a if ((o * NT + tc) & 1) == 0 else ps_b
            for d in range(ND):
                p.add("tensor", lambda t, o=o, tc=tc, d=d, bank=bank:
                      t.matmul(bank[:, 0:512],
                               wtT[:, d*1024 + o*128: d*1024 + o*128 + 128],
                               xn_chunk(d)[:, tc*512:(tc+1)*512],
                               start=(d == 0), stop=(d == ND-1)),
                      reads=[wtT[:, d*1024 + o*128: d*1024 + o*128 + 128],
                             xn_chunk(d)[:, tc*512:(tc+1)*512]],
                      writes=[bank], tag=f"{tag}mm")
            xs = xres_chunk(o)[:, tc*512:(tc+1)*512]
            bcol = bias_sb[:, o:o+1]
            p.add("vector", lambda v, bank=bank, bcol=bcol:
                  v.tensor_scalar(C.tmp512[:, 0:512], bank[:, 0:512], acol[:, 0:1],
                                  bcol, ALU.mult, ALU.add),
                  reads=[bank, acol, bcol], writes=[C.tmp512], tag=f"{tag}ev1")
            p.add("vector", lambda v, xs=xs: v.tensor_tensor(xs, xs, C.tmp512[:, 0:512],
                                                             ALU.add),
                  reads=[xs, C.tmp512], writes=[xs], tag=f"{tag}ev2")


def emit_attn_l3(C, *, T, D, xres_chunk, xn_chunk, wqkv_dram, wout_dram, bout_sb,
                 qh, kh, ktm, vtm, kv_sb, zcol, ctx, wbuf, den, ps_a, ps_b, ps_c,
                 tag=""):
    p = C.prog
    ND = D // 128
    NT = T // 512
    NH = 16
    NTT = T // 128
    p.add("gpsimd", lambda g: g.memset(vtm[:, 0:NTT*72], 1.0),
          writes=[vtm[:, 0:NTT*72]], tag=f"{tag}v1")
    for h in range(NH):
        for which, dst in ((0, qh), (1, kh), (2, None)):
            col0 = which * D + h * 64
            pp = (h * 3 + which) & 1
            slot = wbuf[:, pp*1024:pp*1024 + ND*64]
            src = wqkv_dram[:, col0:col0+64].rearrange("(c p) o -> p c o", p=128)
            p.dma("gpsimd", "l3w", pp, slot.rearrange("p (c o) -> p c o", o=64), src,
                  tag=f"{tag}wq")
            for tc in range(NT):
                for d in range(ND):
                    p.add("tensor", lambda t, d=d, tc=tc, slot=slot:
                          t.matmul(ps_a[0:64, 0:512], slot[:, d*64:(d+1)*64],
                                   xn_chunk(d)[:, tc*512:(tc+1)*512],
                                   start=(d == 0), stop=(d == ND-1)),
                          reads=[slot[:, d*64:(d+1)*64], xn_chunk(d)[:, tc*512:(tc+1)*512]],
                          writes=[ps_a], tag=f"{tag}pj")
                if which < 2:
                    p.add("vector", lambda v, tc=tc: v.tensor_scalar_min(
                              C.tmp512[:, 0:512][0:64, :], ps_a[0:64, 0:512], 0.0),
                          reads=[ps_a], writes=[C.tmp512], tag=f"{tag}mn")
                    p.add("scalar", lambda s, tc=tc: s.activation(
                              C.tmp512c[0:64, 0:512], C.tmp512[0:64, 0:512], AF.Exp),
                          reads=[C.tmp512], writes=[C.tmp512c], tag=f"{tag}ex")
                    p.add("vector", lambda v, tc=tc: v.tensor_relu(
                              C.tmp512[0:64, 0:512], ps_a[0:64, 0:512]),
                          reads=[ps_a], writes=[C.tmp512], tag=f"{tag}rl")
                    p.add("vector", lambda v, tc=tc, dst=dst: v.tensor_tensor(
                              dst[0:64, tc*512:(tc+1)*512], C.tmp512[0:64, 0:512],
                              C.tmp512c[0:64, 0:512], ALU.add),
                          reads=[C.tmp512, C.tmp512c],
                          writes=[dst[0:64, tc*512:(tc+1)*512]], tag=f"{tag}el")
                else:
                    for i in range(4):
                        tt = tc*4 + i
                        p.add("vector", lambda v, i=i: v.tensor_copy(
                                  C.tmp512b[0:64, i*128:(i+1)*128],
                                  ps_a[0:64, i*128:(i+1)*128]),
                              reads=[ps_a], writes=[C.tmp512b], tag=f"{tag}vc")
                        p.add("tensor", lambda t, i=i: t.transpose(
                                  C.ps_t[:, 0:64], C.tmp512b[0:64, i*128:(i+1)*128],
                                  C.ident_bf[0:64, 0:64]),
                              reads=[C.tmp512b, C.ident_bf], writes=[C.ps_t],
                              tag=f"{tag}vt")
                        p.add("vector", lambda v, tt=tt: v.tensor_copy(
                                  vtm[:, tt*72:tt*72+64], C.ps_t[:, 0:64]),
                              reads=[C.ps_t], writes=[vtm[:, tt*72:tt*72+64]],
                              tag=f"{tag}ve")
        for tt in range(NTT):
            p.add("tensor", lambda t, tt=tt: t.transpose(
                      C.ps_t[:, 0:64], kh[0:64, tt*128:(tt+1)*128],
                      C.ident_bf[0:64, 0:64]),
                  reads=[kh[0:64, tt*128:(tt+1)*128], C.ident_bf], writes=[C.ps_t],
                  tag=f"{tag}kt")
            p.add("vector", lambda v, tt=tt: v.tensor_copy(
                      ktm[:, tt*72:tt*72+64], C.ps_t[:, 0:64]),
                  reads=[C.ps_t], writes=[ktm[:, tt*72:tt*72+64]], tag=f"{tag}ke")
        for tt in range(NTT):
            p.add("tensor", lambda t, tt=tt: t.matmul(
                      ps_b[0:64, 0:65], ktm[:, tt*72:tt*72+64], vtm[:, tt*72:tt*72+65],
                      start=(tt == 0), stop=(tt == NTT-1)),
                  reads=[ktm[:, tt*72:tt*72+64], vtm[:, tt*72:tt*72+65]],
                  writes=[ps_b], tag=f"{tag}kv")
        p.add("vector", lambda v: v.tensor_copy(kv_sb[0:64, 0:65], ps_b[0:64, 0:65]),
              reads=[ps_b], writes=[kv_sb], tag=f"{tag}kve")
        for tc in range(NT):
            p.add("tensor", lambda t, tc=tc: t.matmul(
                      ps_a[0:1, 0:512], kv_sb[0:64, 64:65], qh[0:64, tc*512:(tc+1)*512],
                      start=True, stop=True),
                  reads=[kv_sb[0:64, 64:65], qh[0:64, tc*512:(tc+1)*512]],
                  writes=[ps_a], tag=f"{tag}dn")
            dsl = den[0:1, 0:512]
            p.add("vector", lambda v, dsl=dsl: v.tensor_scalar_add(dsl, ps_a[0:1, 0:512],
                                                                   1e-6),
                  reads=[ps_a], writes=[dsl], tag=f"{tag}dna")
            p.add("vector", lambda v, dsl=dsl: v.reciprocal(dsl, dsl),
                  reads=[dsl], writes=[dsl], tag=f"{tag}dnr")
            p.add("tensor", lambda t, dsl=dsl: t.matmul(ps_a[:, 0:512], C.ones_k1, dsl,
                                                        start=True, stop=True),
                  reads=[C.ones_k1, dsl], writes=[ps_a], tag=f"{tag}dnb")
            p.add("vector", lambda v, tc=tc: v.tensor_tensor(
                      C.tmp512b[0:64, 0:512], qh[0:64, tc*512:(tc+1)*512],
                      ps_a[0:64, 0:512], ALU.mult),
                  reads=[qh[0:64, tc*512:(tc+1)*512], ps_a], writes=[C.tmp512b],
                  tag=f"{tag}qs")
            hc = h // 2
            ho = (h % 2) * 64
            p.add("tensor", lambda t, tc=tc, ho=ho: t.matmul(
                      ps_c[ho:ho+64, 0:512], kv_sb[0:64, 0:64], C.tmp512b[0:64, 0:512],
                      start=True, stop=True, tile_position=(0, ho)),
                  reads=[kv_sb[0:64, 0:64], C.tmp512b], writes=[ps_c], tag=f"{tag}cx")
            p.add("vector", lambda v, hc=hc, ho=ho, tc=tc: v.tensor_copy(
                      ctx[ho:ho+64, hc*T + tc*512: hc*T + tc*512 + 512],
                      ps_c[ho:ho+64, 0:512]),
                  reads=[ps_c], writes=[ctx[ho:ho+64, hc*T + tc*512: hc*T + tc*512 + 512]],
                  tag=f"{tag}cxe")
    for o in range(ND):
        pp = o & 1
        slot = wbuf[:, pp*1024:pp*1024 + ND*128]
        src = wout_dram[:, o*128:(o+1)*128].rearrange("(c p) o -> p c o", p=128)
        p.dma("gpsimd", "l3wo", pp, slot.rearrange("p (c o) -> p c o", o=128), src,
              tag=f"{tag}wo")
        for tc in range(NT):
            bank = ps_a if ((o * NT + tc) & 1) == 0 else ps_b
            for c in range(ND):
                p.add("tensor", lambda t, o=o, tc=tc, c=c, slot=slot, bank=bank:
                      t.matmul(bank[:, 0:512], slot[:, c*128:(c+1)*128],
                               ctx[:, c*T + tc*512: c*T + tc*512 + 512],
                               start=(c == 0), stop=(c == ND-1)),
                      reads=[slot[:, c*128:(c+1)*128],
                             ctx[:, c*T + tc*512: c*T + tc*512 + 512]],
                      writes=[bank], tag=f"{tag}om")
            xs = xres_chunk(o)[:, tc*512:(tc+1)*512]
            bcol = bout_sb[:, o:o+1]
            p.add("vector", lambda v, bank=bank, bcol=bcol, xs=xs:
                  v.scalar_tensor_tensor(xs, bank[:, 0:512], bcol, xs, ALU.add, ALU.add),
                  reads=[bank, bcol, xs], writes=[xs], tag=f"{tag}oe")


def emit_attn_l1(C, *, T, D, xres_chunk, xn_chunk, w_dram, b_sb,
                 kfm, qfm, et_sb, cells, qrow, wbuf, vst, vtm_dram, nst_dram,
                 ps_a, ps_b, ps_c, tag=""):
    p = C.prog
    ND = D // 128
    NT = T // 512
    onec = C.ones_col
    KM2, QM2, CC, ZAC, NEG = 0, 1, 2, 3, 4
    for which, name in ((1, "k"), (2, "v")):
        for o in range(ND):
            pp = o & 1
            slot = wbuf[:, pp*1024:pp*1024 + ND*128]
            src = w_dram(which)[:, o*128:(o+1)*128].rearrange("(c p) o -> p c o", p=128)
            p.dma("gpsimd", "l1w" + name, pp, slot.rearrange("p (c o) -> p c o", o=128),
                  src, tag=f"{tag}w{name}")
            for tc in range(NT):
                for d in range(ND):
                    p.add("tensor", lambda t, d=d, tc=tc, slot=slot:
                          t.matmul(ps_a[:, 0:512], slot[:, d*128:(d+1)*128],
                                   xn_chunk(d)[:, tc*512:(tc+1)*512],
                                   start=(d == 0), stop=(d == ND-1)),
                          reads=[slot[:, d*128:(d+1)*128],
                                 xn_chunk(d)[:, tc*512:(tc+1)*512]],
                          writes=[ps_a], tag=f"{tag}p{name}")
                bcol = b_sb[:, which*ND + o: which*ND + o + 1]
                if which == 1:
                    p.add("vector", lambda v, o=o, tc=tc, bcol=bcol: v.tensor_scalar(
                              kfm[:, o*T + tc*512: o*T + tc*512 + 512], ps_a[:, 0:512],
                              bcol, None, ALU.add),
                          reads=[ps_a, bcol],
                          writes=[kfm[:, o*T + tc*512: o*T + tc*512 + 512]],
                          tag=f"{tag}ek")
                else:
                    p.add("vector", lambda v, bcol=bcol: v.tensor_scalar(
                              C.tmp512b[:, 0:512], ps_a[:, 0:512], bcol, None, ALU.add),
                          reads=[ps_a, bcol], writes=[C.tmp512b], tag=f"{tag}ev")
                    for i in range(4):
                        st = tc*4 + i
                        p.add("tensor", lambda t, i=i: t.transpose(
                                  C.ps_t[:, 0:128], C.tmp512b[:, i*128:(i+1)*128],
                                  C.ident_bf),
                              reads=[C.tmp512b, C.ident_bf], writes=[C.ps_t],
                              tag=f"{tag}vt")
                        vs = vst[0][:, (st & 3)*128 + 0: (st & 3)*128 + 128]
                        p.add("vector", lambda v, vs=vs: v.tensor_copy(
                                  vs, C.ps_t[:, 0:128]),
                              reads=[C.ps_t], writes=[vs], tag=f"{tag}vc")
                        p.dma("sync", "l1vo", st & 1,
                              vtm_dram[st][:, o*128:(o+1)*128], vs, tag=f"{tag}vd")
    for tc in range(NT):
        for o in range(ND):
            p.add("scalar", lambda s, o=o, tc=tc: s.activation(
                      C.tmp512[:, 0:512], kfm[:, o*T + tc*512: o*T + tc*512 + 512],
                      AF.Square),
                  reads=[kfm[:, o*T + tc*512: o*T + tc*512 + 512]], writes=[C.tmp512],
                  tag=f"{tag}ks")
            p.add("tensor", lambda t, o=o: t.matmul(ps_b[0:1, 0:512], onec,
                                                    C.tmp512[:, 0:512],
                                                    start=(o == 0), stop=(o == ND-1)),
                  reads=[onec, C.tmp512], writes=[ps_b], tag=f"{tag}kn")
        p.add("vector", lambda v: v.reduce_max(cells[0:1, 8:9], ps_b[0:1, 0:512], AX.X),
              reads=[ps_b], writes=[cells[0:1, 8:9]], tag=f"{tag}knm")
        if tc == 0:
            p.add("vector", lambda v: v.tensor_copy(cells[0:1, KM2:KM2+1],
                                                    cells[0:1, 8:9]),
                  reads=[cells[0:1, 8:9]], writes=[cells[0:1, KM2:KM2+1]],
                  tag=f"{tag}knc")
        else:
            p.add("vector", lambda v: v.tensor_tensor(cells[0:1, KM2:KM2+1],
                                                      cells[0:1, KM2:KM2+1],
                                                      cells[0:1, 8:9], ALU.max),
                  reads=[cells[0:1, KM2:KM2+1], cells[0:1, 8:9]],
                  writes=[cells[0:1, KM2:KM2+1]], tag=f"{tag}knc")
    for o in range(ND):
        pp = o & 1
        slot = wbuf[:, pp*1024:pp*1024 + ND*128]
        src = w_dram(0)[:, o*128:(o+1)*128].rearrange("(c p) o -> p c o", p=128)
        p.dma("gpsimd", "l1wq", pp, slot.rearrange("p (c o) -> p c o", o=128), src,
              tag=f"{tag}wq")
        for tc in range(NT):
            for d in range(ND):
                p.add("tensor", lambda t, d=d, tc=tc, slot=slot:
                      t.matmul(ps_a[:, 0:512], slot[:, d*128:(d+1)*128],
                               xn_chunk(d)[:, tc*512:(tc+1)*512],
                               start=(d == 0), stop=(d == ND-1)),
                      reads=[slot[:, d*128:(d+1)*128], xn_chunk(d)[:, tc*512:(tc+1)*512]],
                      writes=[ps_a], tag=f"{tag}q1")
            bcol = b_sb[:, o:o+1]
            p.add("vector", lambda v, bcol=bcol: v.tensor_scalar(
                      C.tmp512b[:, 0:512], ps_a[:, 0:512], bcol, None, ALU.add),
                  reads=[ps_a, bcol], writes=[C.tmp512b], tag=f"{tag}q1e")
            p.add("scalar", lambda s: s.activation(C.tmp512[:, 0:512],
                                                   C.tmp512b[:, 0:512], AF.Square),
                  reads=[C.tmp512b], writes=[C.tmp512], tag=f"{tag}q1s")
            p.add("tensor", lambda t: t.matmul(ps_b[0:1, 0:512], onec,
                                               C.tmp512[:, 0:512],
                                               start=True, stop=True),
                  reads=[onec, C.tmp512], writes=[ps_b], tag=f"{tag}q1n")
            if o == 0:
                p.add("vector", lambda v, tc=tc: v.tensor_copy(
                          qrow[0:1, tc*512:(tc+1)*512], ps_b[0:1, 0:512]),
                      reads=[ps_b], writes=[qrow[0:1, tc*512:(tc+1)*512]],
                      tag=f"{tag}q1a")
            else:
                p.add("vector", lambda v, tc=tc: v.tensor_tensor(
                          qrow[0:1, tc*512:(tc+1)*512],
                          qrow[0:1, tc*512:(tc+1)*512], ps_b[0:1, 0:512], ALU.add),
                      reads=[qrow[0:1, tc*512:(tc+1)*512], ps_b],
                      writes=[qrow[0:1, tc*512:(tc+1)*512]], tag=f"{tag}q1a")
    p.add("vector", lambda v: v.reduce_max(cells[0:1, QM2:QM2+1], qrow[0:1, 0:T],
                                           AX.X),
          reads=[qrow[0:1, 0:T]], writes=[cells[0:1, QM2:QM2+1]], tag=f"{tag}qm")
    p.add("vector", lambda v: v.tensor_tensor(cells[0:1, CC:CC+1],
                                              cells[0:1, KM2:KM2+1],
                                              cells[0:1, QM2:QM2+1], ALU.mult),
          reads=[cells[0:1, KM2:KM2+2]], writes=[cells[0:1, CC:CC+1]], tag=f"{tag}c1")
    p.add("scalar", lambda s: s.activation(cells[0:1, CC:CC+1], cells[0:1, CC:CC+1],
                                           AF.Sqrt),
          reads=[cells[0:1, CC:CC+1]], writes=[cells[0:1, CC:CC+1]], tag=f"{tag}c2")
    p.add("vector", lambda v: v.tensor_scalar_mul(cells[0:1, NEG:NEG+1],
                                                  cells[0:1, CC:CC+1], -1.0 / 32.0),
          reads=[cells[0:1, CC:CC+1]], writes=[cells[0:1, NEG:NEG+1]], tag=f"{tag}c3")
    p.add("tensor", lambda t: t.matmul(ps_c[:, 0:1], C.ones_k1,
                                       cells[0:1, NEG:NEG+1], start=True, stop=True),
          reads=[C.ones_k1, cells[0:1, NEG:NEG+1]], writes=[ps_c], tag=f"{tag}c4")
    negc = C.misc["l1negc"]
    p.add("vector", lambda v: v.tensor_copy(negc[:, 0:1], ps_c[:, 0:1]),
          reads=[ps_c], writes=[negc], tag=f"{tag}c5")
    p.add("gpsimd", lambda g: g.memset(cells[0:1, ZAC:ZAC+1], 0.0),
          writes=[cells[0:1, ZAC:ZAC+1]], tag=f"{tag}zz")
    for tc in range(NT):
        for o in range(ND):
            pp = o & 1
            slot = wbuf[:, pp*1024:pp*1024 + ND*128]
            src = w_dram(0)[:, o*128:(o+1)*128].rearrange("(c p) o -> p c o", p=128)
            p.dma("gpsimd", "l1wq2", pp, slot.rearrange("p (c o) -> p c o", o=128), src,
                  tag=f"{tag}wq2")
            for d in range(ND):
                p.add("tensor", lambda t, d=d, tc=tc, slot=slot:
                      t.matmul(ps_a[:, 0:512], slot[:, d*128:(d+1)*128],
                               xn_chunk(d)[:, tc*512:(tc+1)*512],
                               start=(d == 0), stop=(d == ND-1)),
                      reads=[slot[:, d*128:(d+1)*128], xn_chunk(d)[:, tc*512:(tc+1)*512]],
                      writes=[ps_a], tag=f"{tag}q2")
            bcol = b_sb[:, o:o+1]
            p.add("vector", lambda v, o=o, bcol=bcol: v.tensor_scalar(
                      qfm[:, o*512:(o+1)*512], ps_a[:, 0:512], bcol, None, ALU.add),
                  reads=[ps_a, bcol], writes=[qfm[:, o*512:(o+1)*512]], tag=f"{tag}q2e")
        ns = 4 * (tc + 1)
        for st in range(ns):
            bank = ps_a if (st & 1) == 0 else ps_b
            for d in range(ND):
                p.add("tensor", lambda t, st=st, d=d, bank=bank: t.matmul(
                          bank[:, 0:512], kfm[:, d*T + st*128: d*T + st*128 + 128],
                          qfm[:, d*512:(d+1)*512], start=(d == 0), stop=(d == ND-1)),
                      reads=[kfm[:, d*T + st*128: d*T + st*128 + 128],
                             qfm[:, d*512:(d+1)*512]],
                      writes=[bank], tag=f"{tag}S")
            p.add("scalar", lambda s, st=st, bank=bank: s.activation(
                      et_sb[:, st*512:(st+1)*512], bank[:, 0:512], AF.Exp,
                      bias=negc[:, 0:1], scale=1.0 / 32.0),
                  reads=[bank, negc], writes=[et_sb[:, st*512:(st+1)*512]],
                  tag=f"{tag}E")
            if st >= 4 * tc:
                i = st - 4 * tc
                p.add("vector", lambda v, st=st, i=i: v.tensor_tensor(
                          et_sb[:, st*512 + i*128: st*512 + i*128 + 128],
                          et_sb[:, st*512 + i*128: st*512 + i*128 + 128],
                          C.tri_bf, ALU.mult),
                      reads=[et_sb[:, st*512 + i*128: st*512 + i*128 + 128], C.tri_bf],
                      writes=[et_sb[:, st*512 + i*128: st*512 + i*128 + 128]],
                      tag=f"{tag}msk")
                for j in range(i + 1, 4):
                    p.add("gpsimd", lambda g, st=st, j=j: g.memset(
                              et_sb[:, st*512 + j*128: st*512 + j*128 + 128], 0.0),
                          writes=[et_sb[:, st*512 + j*128: st*512 + j*128 + 128]],
                          tag=f"{tag}msz")
        for st in range(ns):
            p.add("tensor", lambda t, st=st, ns=ns: t.matmul(
                      ps_c[0:1, 0:512], C.ones_col_bf,
                      et_sb[:, st*512:(st+1)*512], start=(st == 0), stop=(st == ns-1)),
                  reads=[C.ones_col_bf, et_sb[:, st*512:(st+1)*512]], writes=[ps_c],
                  tag=f"{tag}Z")
        p.add("vector", lambda v: v.tensor_reduce(cells[0:1, 9:10], ps_c[0:1, 0:512],
                                                  AX.X, ALU.add),
              reads=[ps_c], writes=[cells[0:1, 9:10]], tag=f"{tag}Zr")
        p.add("vector", lambda v: v.tensor_tensor(cells[0:1, ZAC:ZAC+1],
                                                  cells[0:1, ZAC:ZAC+1],
                                                  cells[0:1, 9:10], ALU.add),
              reads=[cells[0:1, ZAC:ZAC+1], cells[0:1, 9:10]],
              writes=[cells[0:1, ZAC:ZAC+1]], tag=f"{tag}Za")
        for o in range(ND):
            bank = ps_a if (o & 1) == 0 else ps_b
            pp = o & 1
            p.dma("sync", "l1vi", pp, vst[1][:, pp*2048:pp*2048+2048].rearrange(
                      "p (s o) -> p s o", o=128),
                  vtm_dram.rearrange("s p o -> p s o")[:, :, o*128:(o+1)*128],
                  tag=f"{tag}vl")
            for st in range(ns):
                p.add("tensor", lambda t, o=o, st=st, bank=bank, pp=pp, ns=ns: t.matmul(
                          bank[:, 0:512],
                          vst[1][:, pp*2048 + st*128: pp*2048 + st*128 + 128],
                          et_sb[:, st*512:(st+1)*512], start=(st == 0),
                          stop=(st == ns-1)),
                      reads=[vst[1][:, pp*2048 + st*128: pp*2048 + st*128 + 128],
                             et_sb[:, st*512:(st+1)*512]],
                      writes=[bank], tag=f"{tag}N")
            p.add("vector", lambda v, bank=bank: v.tensor_copy(
                      C.tmp512[:, 0:512], bank[:, 0:512]),
                  reads=[bank], writes=[C.tmp512], tag=f"{tag}Nc")
            p.dma("sync", "l1nst", (o*NT + tc) & 1, nst_dram[o*NT + tc],
                  C.tmp512[:, 0:512], tag=f"{tag}Ns")
    p.add("vector", lambda v: v.reciprocal(cells[0:1, 10:11], cells[0:1, ZAC:ZAC+1]),
          reads=[cells[0:1, ZAC:ZAC+1]], writes=[cells[0:1, 10:11]], tag=f"{tag}Zv")
    p.add("tensor", lambda t: t.matmul(ps_c[:, 0:1], C.ones_k1, cells[0:1, 10:11],
                                       start=True, stop=True),
          reads=[C.ones_k1, cells[0:1, 10:11]], writes=[ps_c], tag=f"{tag}Zb")
    rz = C.misc["l1rz"]
    p.add("vector", lambda v: v.tensor_copy(rz[:, 0:1], ps_c[:, 0:1]),
          reads=[ps_c], writes=[rz], tag=f"{tag}Zc")
    for o in range(ND):
        for tc in range(NT):
            xs = xres_chunk(o)[:, tc*512:(tc+1)*512]
            buf = C.tmp512[:, 0:512]
            p.dma("sync", "l1nld", (o*NT + tc) & 1, buf, nst_dram[o*NT + tc],
                  tag=f"{tag}Nl")
            p.add("vector", lambda v, xs=xs, buf=buf: v.scalar_tensor_tensor(
                      xs, buf, rz[:, 0:1], xs, ALU.mult, ALU.add),
                  reads=[buf, rz, xs], writes=[xs], tag=f"{tag}out")


def emit_attn_l2(C, *, T, D, xres_chunk, xn_chunk, w_dram, b_sb, wo_dram, bo_sb,
                 qh, kh, vtm, et_sb, zrow, ctx, wbuf, ps_a, ps_b, ps_c, tag=""):
    p = C.prog
    ND = D // 128
    NT = T // 512
    NS = T // 128
    NH = 16
    onec = C.ones_col
    for h in range(NH):
        for which, dst in ((0, qh), (1, kh), (2, None)):
            col0 = h * 64
            pp = (h * 3 + which) & 1
            slot = wbuf[:, pp*1024:pp*1024 + ND*64]
            src = w_dram(which)[:, col0:col0+64].rearrange("(c p) o -> p c o", p=128)
            p.dma("gpsimd", "l2aw", pp, slot.rearrange("p (c o) -> p c o", o=64), src,
                  tag=f"{tag}w")
            bcol = b_sb[0:64, which*NH + h: which*NH + h + 1]
            for tc in range(NT):
                for d in range(ND):
                    p.add("tensor", lambda t, d=d, tc=tc, slot=slot:
                          t.matmul(ps_a[0:64, 0:512], slot[:, d*64:(d+1)*64],
                                   xn_chunk(d)[:, tc*512:(tc+1)*512],
                                   start=(d == 0), stop=(d == ND-1)),
                          reads=[slot[:, d*64:(d+1)*64],
                                 xn_chunk(d)[:, tc*512:(tc+1)*512]],
                          writes=[ps_a], tag=f"{tag}pj")
                if which < 2:
                    p.add("vector", lambda v, tc=tc, dst=dst, bcol=bcol: v.tensor_scalar(
                              dst[0:64, tc*512:(tc+1)*512], ps_a[0:64, 0:512],
                              bcol, None, ALU.add),
                          reads=[ps_a, bcol], writes=[dst[0:64, tc*512:(tc+1)*512]],
                          tag=f"{tag}pe")
                else:
                    p.add("vector", lambda v, bcol=bcol: v.tensor_scalar(
                              C.tmp512b[0:64, 0:512], ps_a[0:64, 0:512],
                              bcol, None, ALU.add),
                          reads=[ps_a, bcol], writes=[C.tmp512b], tag=f"{tag}pv")
                    for i in range(4):
                        st = tc*4 + i
                        p.add("tensor", lambda t, i=i: t.transpose(
                                  C.ps_t[:, 0:64], C.tmp512b[0:64, i*128:(i+1)*128],
                                  C.ident_bf[0:64, 0:64]),
                              reads=[C.tmp512b, C.ident_bf], writes=[C.ps_t],
                              tag=f"{tag}vt")
                        p.add("vector", lambda v, st=st: v.tensor_copy(
                                  vtm[:, st*64:(st+1)*64], C.ps_t[:, 0:64]),
                              reads=[C.ps_t], writes=[vtm[:, st*64:(st+1)*64]],
                              tag=f"{tag}ve")
        for which, dst in ((1, qh), (0, kh)):
            cell = zrow[0:1, 512 + which: 513 + which]
            for tc in range(NT):
                p.add("scalar", lambda s, tc=tc, dst=dst: s.activation(
                          C.tmp512[0:64, 0:512], dst[0:64, tc*512:(tc+1)*512],
                          AF.Square),
                      reads=[dst[0:64, tc*512:(tc+1)*512]], writes=[C.tmp512],
                      tag=f"{tag}ns")
                p.add("tensor", lambda t: t.matmul(ps_b[0:1, 0:512], onec[0:64, 0:1],
                                                   C.tmp512[0:64, 0:512],
                                                   start=True, stop=True),
                      reads=[onec, C.tmp512], writes=[ps_b], tag=f"{tag}nm")
                p.add("vector", lambda v: v.reduce_max(zrow[0:1, 516:517],
                                                       ps_b[0:1, 0:512], AX.X),
                      reads=[ps_b], writes=[zrow[0:1, 516:517]], tag=f"{tag}nr")
                if tc == 0:
                    p.add("vector", lambda v, cell=cell: v.tensor_copy(
                              cell, zrow[0:1, 516:517]),
                          reads=[zrow[0:1, 516:517]], writes=[cell], tag=f"{tag}nc")
                else:
                    p.add("vector", lambda v, cell=cell: v.tensor_tensor(
                              cell, cell, zrow[0:1, 516:517], ALU.max),
                          reads=[cell, zrow[0:1, 516:517]], writes=[cell],
                          tag=f"{tag}nc")
        p.add("vector", lambda v: v.tensor_tensor(zrow[0:1, 514:515],
                                                  zrow[0:1, 512:513],
                                                  zrow[0:1, 513:514], ALU.mult),
              reads=[zrow[0:1, 512:514]], writes=[zrow[0:1, 514:515]],
              tag=f"{tag}n3")
        p.add("scalar", lambda s: s.activation(zrow[0:1, 514:515],
                                               zrow[0:1, 514:515], AF.Sqrt),
              reads=[zrow[0:1, 514:515]], writes=[zrow[0:1, 514:515]],
              tag=f"{tag}n4")
        p.add("vector", lambda v: v.tensor_scalar_mul(zrow[0:1, 515:516],
                                                      zrow[0:1, 514:515], -1.0 / 8.0),
              reads=[zrow[0:1, 514:515]], writes=[zrow[0:1, 515:516]],
              tag=f"{tag}n5")
        p.add("tensor", lambda t: t.matmul(C.ps_t32[:, 0:1], C.ones_k1,
                                           zrow[0:1, 515:516],
                                           start=True, stop=True),
              reads=[C.ones_k1, zrow[0:1, 515:516]], writes=[C.ps_t32], tag=f"{tag}n6")
        negc = C.misc["l1negc"]
        p.add("vector", lambda v: v.tensor_copy(negc[:, 0:1], C.ps_t32[:, 0:1]),
              reads=[C.ps_t32], writes=[negc], tag=f"{tag}n7")
        for tc in range(NT):
            ns = 4 * (tc + 1)
            for st in range(ns):
                bank = ps_a if (st & 1) == 0 else ps_b
                p.add("tensor", lambda t, st=st, tc=tc, bank=bank: t.matmul(
                          bank[:, 0:512], kh[0:64, st*128:(st+1)*128],
                          qh[0:64, tc*512:(tc+1)*512], start=True, stop=True),
                      reads=[kh[0:64, st*128:(st+1)*128], qh[0:64, tc*512:(tc+1)*512]],
                      writes=[bank], tag=f"{tag}S")
                p.add("scalar", lambda s, st=st, bank=bank: s.activation(
                          et_sb[:, st*512:(st+1)*512], bank[:, 0:512], AF.Exp,
                          bias=negc[:, 0:1], scale=1.0 / 8.0),
                      reads=[bank, negc], writes=[et_sb[:, st*512:(st+1)*512]],
                      tag=f"{tag}E")
                if st >= 4 * tc:
                    i = st - 4 * tc
                    p.add("vector", lambda v, st=st, i=i: v.tensor_tensor(
                              et_sb[:, st*512 + i*128: st*512 + i*128 + 128],
                              et_sb[:, st*512 + i*128: st*512 + i*128 + 128],
                              C.tri_bf, ALU.mult),
                          reads=[et_sb[:, st*512 + i*128: st*512 + i*128 + 128],
                                 C.tri_bf],
                          writes=[et_sb[:, st*512 + i*128: st*512 + i*128 + 128]],
                          tag=f"{tag}msk")
                    for j in range(i + 1, 4):
                        p.add("gpsimd", lambda g, st=st, j=j: g.memset(
                                  et_sb[:, st*512 + j*128: st*512 + j*128 + 128], 0.0),
                              writes=[et_sb[:, st*512 + j*128: st*512 + j*128 + 128]],
                              tag=f"{tag}msz")
            for st in range(ns):
                p.add("tensor", lambda t, st=st, ns=ns: t.matmul(
                          ps_c[0:1, 0:512], C.ones_col_bf, et_sb[:, st*512:(st+1)*512],
                          start=(st == 0), stop=(st == ns-1)),
                      reads=[C.ones_col_bf, et_sb[:, st*512:(st+1)*512]], writes=[ps_c],
                      tag=f"{tag}Z")
            p.add("vector", lambda v: v.reciprocal(zrow[0:1, 0:512],
                                                   ps_c[0:1, 0:512]),
                  reads=[ps_c], writes=[zrow[0:1, 0:512]], tag=f"{tag}Zr")
            p.add("tensor", lambda t: t.matmul(C.ps_t32[:, 0:512], C.ones_k1,
                                               zrow[0:1, 0:512],
                                               start=True, stop=True),
                  reads=[C.ones_k1, zrow[0:1, 0:512]], writes=[C.ps_t32],
                  tag=f"{tag}Zb")
            bank = ps_a if (tc & 1) == 0 else ps_b
            hc = h // 2
            ho = (h % 2) * 64
            for st in range(ns):
                p.add("tensor", lambda t, st=st, bank=bank, ns=ns, ho=ho: t.matmul(
                          bank[ho:ho+64, 0:512], vtm[:, st*64:(st+1)*64],
                          et_sb[:, st*512:(st+1)*512], start=(st == 0),
                          stop=(st == ns-1), tile_position=(0, ho)),
                      reads=[vtm[:, st*64:(st+1)*64], et_sb[:, st*512:(st+1)*512]],
                      writes=[bank], tag=f"{tag}cx")
            p.add("scalar", lambda s, bank=bank, ho=ho: s.copy(
                      C.tmp512[ho:ho+64, 0:512], bank[ho:ho+64, 0:512]),
                  reads=[bank], writes=[C.tmp512], tag=f"{tag}ce")
            p.add("vector", lambda v, hc=hc, ho=ho, tc=tc: v.tensor_tensor(
                      ctx[ho:ho+64, hc*T + tc*512: hc*T + tc*512 + 512],
                      C.tmp512[ho:ho+64, 0:512], C.ps_t32[ho:ho+64, 0:512], ALU.mult),
                  reads=[C.tmp512, C.ps_t32],
                  writes=[ctx[ho:ho+64, hc*T + tc*512: hc*T + tc*512 + 512]],
                  tag=f"{tag}cm")
    for o in range(ND):
        pp = o & 1
        slot = wbuf[:, pp*1024:pp*1024 + ND*128]
        src = wo_dram[:, o*128:(o+1)*128].rearrange("(c p) o -> p c o", p=128)
        p.dma("gpsimd", "l2awo", pp, slot.rearrange("p (c o) -> p c o", o=128), src,
              tag=f"{tag}wo")
        for tc in range(NT):
            bank = ps_a if ((o * NT + tc) & 1) == 0 else ps_b
            for c in range(ND):
                p.add("tensor", lambda t, tc=tc, c=c, slot=slot, bank=bank:
                      t.matmul(bank[:, 0:512], slot[:, c*128:(c+1)*128],
                               ctx[:, c*T + tc*512: c*T + tc*512 + 512],
                               start=(c == 0), stop=(c == ND-1)),
                      reads=[slot[:, c*128:(c+1)*128],
                             ctx[:, c*T + tc*512: c*T + tc*512 + 512]],
                      writes=[bank], tag=f"{tag}om")
            xs = xres_chunk(o)[:, tc*512:(tc+1)*512]
            bcol = bo_sb[:, o:o+1]
            p.add("vector", lambda v, bank=bank, bcol=bcol, xs=xs:
                  v.scalar_tensor_tensor(xs, bank[:, 0:512], bcol, xs, ALU.add, ALU.add),
                  reads=[bank, bcol, xs], writes=[xs], tag=f"{tag}oe")


def _l0_tiles():
    ts = [(i * 128, 128) for i in range(21)] + [(2688, 64)]
    return ts


def emit_ffn_l0(C, *, T, D, xres_chunk, xn_chunk, wg_dram, wu_dram, wd_dram,
                wrow_src, wr_stage, h_sb, gu_buf, wd_buf, wb_sb, tmp_sb,
                ones_f32, ps_a, ps_b, ps_c, ps_d, tag=""):
    p = C.prog
    TH = T // 2
    ND = D // 128
    NTC = TH // 512
    tiles = _l0_tiles()
    E = 9
    for half in range(2):
        t0 = half * TH
        for e in range(E):
            if e < 8:
                p.dma("gpsimd", "l0wr" + tag, 0, wr_stage, wrow_src(e), tag=f"{tag}wr")
                for tc in range(NTC):
                    p.add("tensor", lambda t, tc=tc, t0=t0: t.matmul(
                              ps_d[:, 0:512], ones_f32,
                              wr_stage[0:1, t0 + tc*512: t0 + tc*512 + 512],
                              start=True, stop=True),
                          reads=[ones_f32, wr_stage], writes=[ps_d], tag=f"{tag}wb")
                    p.add("vector", lambda v, tc=tc: v.tensor_copy(
                              wb_sb[:, tc*512:(tc+1)*512], ps_d[:, 0:512]),
                          reads=[ps_d], writes=[wb_sb[:, tc*512:(tc+1)*512]],
                          tag=f"{tag}wc")
            for it, (h0, hw) in enumerate(tiles):
                pp = it & 1
                slot = gu_buf[pp]
                sg = wg_dram(e)[:, h0:h0+hw].rearrange("(c p) o -> p c o", p=128)
                su = wu_dram(e)[:, h0:h0+hw].rearrange("(c p) o -> p c o", p=128)
                p.dma("gpsimd", "l0g", pp, slot[:, 0:ND*hw].rearrange(
                          "p (c o) -> p c o", o=hw), sg, tag=f"{tag}gd")
                p.dma("gpsimd", "l0u", pp, slot[:, ND*128:ND*128 + ND*hw].rearrange(
                          "p (c o) -> p c o", o=hw), su, tag=f"{tag}ud")
                for tc in range(NTC):
                    for d in range(ND):
                        p.add("tensor", lambda t, d=d, tc=tc, slot=slot, hw=hw, t0=t0:
                              t.matmul(ps_a[0:hw, 0:512], slot[:, d*hw:(d+1)*hw],
                                       xn_chunk(d)[:, t0 + tc*512: t0 + tc*512 + 512],
                                       start=(d == 0), stop=(d == ND-1)),
                              reads=[slot[:, d*hw:(d+1)*hw],
                                     xn_chunk(d)[:, t0 + tc*512: t0 + tc*512 + 512]],
                              writes=[ps_a], tag=f"{tag}g")
                        p.add("tensor", lambda t, d=d, tc=tc, slot=slot, hw=hw, t0=t0:
                              t.matmul(ps_b[0:hw, 0:512],
                                       slot[:, ND*128 + d*hw: ND*128 + (d+1)*hw],
                                       xn_chunk(d)[:, t0 + tc*512: t0 + tc*512 + 512],
                                       start=(d == 0), stop=(d == ND-1)),
                              reads=[slot[:, ND*128 + d*hw: ND*128 + (d+1)*hw],
                                     xn_chunk(d)[:, t0 + tc*512: t0 + tc*512 + 512]],
                              writes=[ps_b], tag=f"{tag}u")
                    hsl = h_sb[:, it*TH + tc*512: it*TH + tc*512 + 512]
                    p.add("scalar", lambda s, hsl=hsl, hw=hw: s.activation(
                              hsl[0:hw, :] if hw < 128 else hsl, ps_a[0:hw, 0:512],
                              AF.Silu),
                          reads=[ps_a], writes=[hsl], tag=f"{tag}sl")
                    p.add("scalar", lambda s, hw=hw: s.copy(C.tmp512b[0:hw, 0:512],
                                                            ps_b[0:hw, 0:512]),
                          reads=[ps_b], writes=[C.tmp512b], tag=f"{tag}uc")
                    p.add("vector", lambda v, hsl=hsl, hw=hw: v.tensor_tensor(
                              hsl[0:hw, :] if hw < 128 else hsl,
                              hsl[0:hw, :] if hw < 128 else hsl,
                              C.tmp512b[0:hw, 0:512], ALU.mult),
                          reads=[hsl, C.tmp512b], writes=[hsl], tag=f"{tag}hm")
            for o in range(ND):
                pp = o & 1
                dslot = wd_buf[pp]
                s1 = wd_dram(e)[0:2688, o*128:(o+1)*128].rearrange(
                         "(c p) o -> p c o", p=128)
                p.dma("gpsimd", "l0d", pp, dslot[:, 0:21*128].rearrange(
                          "p (c o) -> p c o", o=128), s1, tag=f"{tag}dd")
                s2 = wd_dram(e)[2688:2752, o*128:(o+1)*128]
                p.dma("gpsimd", "l0d", pp, dslot[0:64, 21*128:22*128], s2,
                      tag=f"{tag}dt")
                for tc in range(NTC):
                    bank = ps_c if (tc & 1) == 0 else ps_d
                    for it, (h0, hw) in enumerate(tiles):
                        p.add("tensor", lambda t, it=it, hw=hw, tc=tc, dslot=dslot,
                              bank=bank:
                              t.matmul(bank[:, 0:512], dslot[0:hw, it*128:it*128+128],
                                       h_sb[0:hw, it*TH + tc*512: it*TH + tc*512 + 512],
                                       start=(it == 0), stop=(it == len(tiles)-1)),
                              reads=[dslot[0:hw, it*128:it*128+128],
                                     h_sb[0:hw, it*TH + tc*512: it*TH + tc*512 + 512]],
                              writes=[bank], tag=f"{tag}dn")
                    xs = xres_chunk(o)[:, t0 + tc*512: t0 + tc*512 + 512]
                    if e < 8:
                        p.add("vector", lambda v, bank=bank, tc=tc: v.tensor_tensor(
                                  tmp_sb[:, 0:512], bank[:, 0:512],
                                  wb_sb[:, tc*512:(tc+1)*512], ALU.mult),
                              reads=[bank, wb_sb[:, tc*512:(tc+1)*512]],
                              writes=[tmp_sb], tag=f"{tag}e1")
                        p.add("vector", lambda v, xs=xs: v.tensor_tensor(
                                  xs, xs, tmp_sb[:, 0:512], ALU.add),
                              reads=[xs, tmp_sb], writes=[xs], tag=f"{tag}e2")
                    else:
                        p.add("vector", lambda v, xs=xs, bank=bank: v.tensor_tensor(
                                  xs, xs, bank[:, 0:512], ALU.add),
                              reads=[xs, bank], writes=[xs], tag=f"{tag}e3")


def emit_logits_tm(C, *, T, D, xn_chunk, wT_sb, width, out_tm, ps, tag=""):
    p = C.prog
    ND = D // 128
    NTT = T // 128
    for tt in range(NTT):
        for d in range(ND):
            p.add("tensor", lambda t, tt=tt, d=d: t.matmul(
                      ps[:, 0:width], xn_chunk(d)[:, tt*128:(tt+1)*128],
                      wT_sb[:, d*width:(d+1)*width],
                      start=(d == 0), stop=(d == ND-1)),
                  reads=[xn_chunk(d)[:, tt*128:(tt+1)*128], wT_sb[:, d*width:(d+1)*width]],
                  writes=[ps], tag=f"{tag}lg")
        p.add("vector", lambda v, tt=tt: v.tensor_copy(
                  out_tm[:, tt*width:(tt+1)*width], ps[:, 0:width]),
              reads=[ps], writes=[out_tm[:, tt*width:(tt+1)*width]], tag=f"{tag}le")


def _route_top2_common(C, scores, sc, i8u, i8f, wcw, tt, renorm_softmax, tag):
    p = C.prog
    v = "vector"
    p.add(v, lambda e: e.max(sc[:, 8:16], scores), reads=[scores], writes=[sc[:, 8:16]],
          tag=tag + "m8")
    p.add(v, lambda e: e.max_index(i8u[:], sc[:, 8:16], scores),
          reads=[sc[:, 8:16], scores], writes=[i8u], tag=tag + "mi")
    p.add(v, lambda e: e.tensor_copy(i8f[:, 0:2], i8u[:, 0:2]),
          reads=[i8u], writes=[i8f], tag=tag + "ic")
    if renorm_softmax == "probs":
        p.add(v, lambda e: e.tensor_scalar(sc[:, 16:18], sc[:, 8:10], sc[:, 4:5], None,
                                           ALU.mult),
              reads=[sc[:, 8:10], sc[:, 4:5]], writes=[sc[:, 16:18]], tag=tag + "p")
        p.add(v, lambda e: e.tensor_tensor(sc[:, 18:19], sc[:, 16:17], sc[:, 17:18],
                                           ALU.add),
              reads=[sc[:, 16:18]], writes=[sc[:, 18:19]], tag=tag + "s")
        p.add(v, lambda e: e.tensor_scalar_add(sc[:, 18:19], sc[:, 18:19], 1e-20),
              reads=[sc[:, 18:19]], writes=[sc[:, 18:19]], tag=tag + "se")
        p.add(v, lambda e: e.reciprocal(sc[:, 19:20], sc[:, 18:19]),
              reads=[sc[:, 18:19]], writes=[sc[:, 19:20]], tag=tag + "sr")
        p.add(v, lambda e: e.tensor_scalar(sc[:, 20:22], sc[:, 16:18], sc[:, 19:20],
                                           None, ALU.mult),
              reads=[sc[:, 16:18], sc[:, 19:20]], writes=[sc[:, 20:22]], tag=tag + "w")
    else:
        p.add(v, lambda e: e.tensor_tensor(sc[:, 16:17], sc[:, 9:10], sc[:, 8:9],
                                           ALU.subtract),
              reads=[sc[:, 8:10]], writes=[sc[:, 16:17]], tag=tag + "d")
        p.add("scalar", lambda e: e.activation(sc[:, 17:18], sc[:, 16:17], AF.Exp),
              reads=[sc[:, 16:17]], writes=[sc[:, 17:18]], tag=tag + "ex")
        p.add(v, lambda e: e.tensor_scalar_add(sc[:, 18:19], sc[:, 17:18], 1.0),
              reads=[sc[:, 17:18]], writes=[sc[:, 18:19]], tag=tag + "a1")
        p.add(v, lambda e: e.reciprocal(sc[:, 20:21], sc[:, 18:19]),
              reads=[sc[:, 18:19]], writes=[sc[:, 20:21]], tag=tag + "r")
        p.add(v, lambda e: e.tensor_tensor(sc[:, 21:22], sc[:, 17:18], sc[:, 20:21],
                                           ALU.mult),
              reads=[sc[:, 17:18], sc[:, 20:21]], writes=[sc[:, 21:22]], tag=tag + "w2")
    p.add(v, lambda e: e.tensor_scalar(sc[:, 24:32], C.iota8, i8f[:, 0:1], None,
                                       ALU.is_equal),
          reads=[C.iota8, i8f[:, 0:1]], writes=[sc[:, 24:32]], tag=tag + "o1")
    p.add(v, lambda e: e.tensor_scalar(wcw, sc[:, 24:32], sc[:, 20:21], None, ALU.mult),
          reads=[sc[:, 24:32], sc[:, 20:21]], writes=[wcw], tag=tag + "cw1")
    p.add(v, lambda e: e.tensor_scalar(sc[:, 24:32], C.iota8, i8f[:, 1:2], None,
                                       ALU.is_equal),
          reads=[C.iota8, i8f[:, 1:2]], writes=[sc[:, 24:32]], tag=tag + "o2")
    p.add(v, lambda e: e.scalar_tensor_tensor(wcw, sc[:, 24:32], sc[:, 21:22], wcw,
                                              ALU.mult, ALU.add),
          reads=[sc[:, 24:32], sc[:, 21:22], wcw], writes=[wcw], tag=tag + "cw2")


def emit_route_l0(C, *, T, D, xn_chunk, gwT_sb, lg_tm, sc, i8u, i8f, cw_tm, wrfm,
                  ps_a, ps_t32, tag=""):
    p = C.prog
    NTT = T // 128
    emit_logits_tm(C, T=T, D=D, xn_chunk=xn_chunk, wT_sb=gwT_sb, width=8,
                   out_tm=lg_tm, ps=ps_a, tag=tag + "lgt")
    for tt in range(NTT):
        lg = lg_tm[:, tt*8:(tt+1)*8]
        p.add("vector", lambda v, lg=lg: v.reduce_max(sc[:, 0:1], lg, AX.X),
              reads=[lg], writes=[sc[:, 0:1]], tag=tag + "rm")
        p.add("vector", lambda v: v.tensor_scalar_mul(sc[:, 1:2], sc[:, 0:1], -1.0),
              reads=[sc[:, 0:1]], writes=[sc[:, 1:2]], tag=tag + "ng")
        p.add("scalar", lambda s, lg=lg: s.activation(sc[:, 32:40], lg, AF.Exp,
                                                      bias=sc[:, 1:2],
                                                      accum_out=sc[:, 3:4]),
              reads=[lg, sc[:, 1:2]], writes=[sc[:, 32:40], sc[:, 3:4]], tag=tag + "ex")
        p.add("vector", lambda v: v.reciprocal(sc[:, 4:5], sc[:, 3:4]),
              reads=[sc[:, 3:4]], writes=[sc[:, 4:5]], tag=tag + "rc")
        wcw = cw_tm[:, tt*8:(tt+1)*8]
        _route_top2_common(C, sc[:, 32:40], sc, i8u, i8f, wcw, tt, "probs", tag + "t2")
        p.add("tensor", lambda t, wcw=wcw: t.transpose(ps_t32[0:8, 0:128], wcw,
                                                       C.ident_f32),
              reads=[wcw, C.ident_f32], writes=[ps_t32], tag=tag + "tp")
        p.add("vector", lambda v, tt=tt: v.tensor_copy(C.wrf_stage[0:8, 0:128],
                                                       ps_t32[0:8, 0:128]),
              reads=[ps_t32], writes=[C.wrf_stage], tag=tag + "we")
        p.dma("sync", "wrf", tt & 1, C.wrfm_dram[:, tt*128:(tt+1)*128],
              C.wrf_stage[0:8, 0:128], tag=tag + "wd")


def emit_route_l1(C, *, T, D, xn_chunk, uzT_sb, onb_dram, lg_tm, tnorm_row, sc, i8u,
                  i8f, cw_tm, wrfm, ps_a, ps_b, ps_t32, tag=""):
    p = C.prog
    ND = D // 128
    NT = T // 512
    NTT = T // 128
    onec = C.ones_col
    emit_logits_tm(C, T=T, D=D, xn_chunk=xn_chunk, wT_sb=uzT_sb, width=8,
                   out_tm=lg_tm, ps=ps_a, tag=tag + "lgt")
    for tc in range(NT):
        for d in range(ND):
            p.add("scalar", lambda s, d=d, tc=tc: s.activation(
                      C.tmp512[:, 0:512], xn_chunk(d)[:, tc*512:(tc+1)*512], AF.Square),
                  reads=[xn_chunk(d)[:, tc*512:(tc+1)*512]], writes=[C.tmp512],
                  tag=tag + "sq")
            p.add("tensor", lambda t, d=d: t.matmul(ps_b[0:1, 0:512], onec,
                                                    C.tmp512[:, 0:512],
                                                    start=(d == 0), stop=(d == ND-1)),
                  reads=[onec, C.tmp512], writes=[ps_b], tag=tag + "nm")
        p.add("scalar", lambda s, tc=tc: s.activation(
                  tnorm_row[0:1, tc*512:(tc+1)*512], ps_b[0:1, 0:512], AF.Sqrt),
              reads=[ps_b], writes=[tnorm_row[0:1, tc*512:(tc+1)*512]], tag=tag + "sx")
        p.add("vector", lambda v, tc=tc: v.tensor_scalar_add(
                  tnorm_row[0:1, tc*512:(tc+1)*512], tnorm_row[0:1, tc*512:(tc+1)*512],
                  1e-8),
              reads=[tnorm_row[0:1, tc*512:(tc+1)*512]],
              writes=[tnorm_row[0:1, tc*512:(tc+1)*512]], tag=tag + "ne")
        p.add("vector", lambda v, tc=tc: v.reciprocal(
                  tnorm_row[0:1, tc*512:(tc+1)*512], tnorm_row[0:1, tc*512:(tc+1)*512]),
              reads=[tnorm_row[0:1, tc*512:(tc+1)*512]],
              writes=[tnorm_row[0:1, tc*512:(tc+1)*512]], tag=tag + "nr")
    for d in range(ND):
        p.add("scalar", lambda s, d=d: s.activation(C.tmp512[:, 0:8],
                                                    uzT_sb[:, d*8:(d+1)*8], AF.Square),
              reads=[uzT_sb[:, d*8:(d+1)*8]], writes=[C.tmp512], tag=tag + "es")
        p.add("tensor", lambda t, d=d: t.matmul(ps_b[0:1, 0:8], onec, C.tmp512[:, 0:8],
                                                start=(d == 0), stop=(d == ND-1)),
              reads=[onec, C.tmp512], writes=[ps_b], tag=tag + "en")
    p.add("scalar", lambda s: s.activation(sc[:, 40:48][0:1, :], ps_b[0:1, 0:8],
                                           AF.Sqrt),
          reads=[ps_b], writes=[sc[:, 40:48]], tag=tag + "eq")
    p.add("vector", lambda v: v.tensor_scalar_add(sc[0:1, 40:48], sc[0:1, 40:48], 1e-8),
          reads=[sc[:, 40:48]], writes=[sc[:, 40:48]], tag=tag + "ea")
    p.add("vector", lambda v: v.reciprocal(sc[0:1, 40:48], sc[0:1, 40:48]),
          reads=[sc[:, 40:48]], writes=[sc[:, 40:48]], tag=tag + "er")
    p.dma("gpsimd", "l1onb", 0, sc[0:1, 48:56], onb_dram, tag=tag + "ob", unique=True)
    p.add("vector", lambda v: v.tensor_scalar_mul(sc[0:1, 48:56], sc[0:1, 48:56], 0.1),
          reads=[sc[0:1, 48:56]], writes=[sc[0:1, 48:56]], tag=tag + "os")
    p.add("tensor", lambda t: t.matmul(ps_b[:, 0:16], C.ones_k1, sc[0:1, 40:56],
                                       start=True, stop=True),
          reads=[C.ones_k1, sc[0:1, 40:56]], writes=[ps_b], tag=tag + "bb")
    bce = C.misc["l1bce"]
    p.add("vector", lambda v: v.tensor_copy(bce[:], ps_b[:, 0:16]),
          reads=[ps_b], writes=[bce], tag=tag + "bc")
    for tt in range(NTT):
        lg = lg_tm[:, tt*8:(tt+1)*8]
        p.add("tensor", lambda t, tt=tt: t.transpose(
                  ps_t32[:, 0:1], tnorm_row[0:1, tt*128:(tt+1)*128],
                  C.ident_f32[0:1, 0:1]),
              reads=[tnorm_row[0:1, tt*128:(tt+1)*128], C.ident_f32], writes=[ps_t32],
              tag=tag + "tt")
        p.add("vector", lambda v: v.tensor_copy(sc[:, 0:1], ps_t32[:, 0:1]),
              reads=[ps_t32], writes=[sc[:, 0:1]], tag=tag + "tc")
        p.add("vector", lambda v, lg=lg: v.tensor_scalar(sc[:, 32:40], lg, sc[:, 0:1],
                                                         None, ALU.mult),
              reads=[lg, sc[:, 0:1]], writes=[sc[:, 32:40]], tag=tag + "s1")
        p.add("vector", lambda v: v.tensor_tensor(sc[:, 32:40], sc[:, 32:40],
                                                  bce[:, 0:8], ALU.mult),
              reads=[sc[:, 32:40], bce], writes=[sc[:, 32:40]], tag=tag + "s2")
        p.add("vector", lambda v: v.tensor_tensor(sc[:, 32:40], sc[:, 32:40],
                                                  bce[:, 8:16], ALU.add),
              reads=[sc[:, 32:40], bce], writes=[sc[:, 32:40]], tag=tag + "s3")
        p.add("vector", lambda v, lg=lg: v.tensor_reduce(sc[:, 0:1], lg, AX.X, ALU.min),
              reads=[lg], writes=[sc[:, 0:1]], tag=tag + "mn")
        p.add("vector", lambda v: v.tensor_scalar(sc[:, 8:16], C.iota8, 0.0,
                                                  sc[:, 0:1], ALU.mult, ALU.add),
              reads=[C.iota8, sc[:, 0:1]], writes=[sc[:, 8:16]], tag=tag + "mb")
        p.add("vector", lambda v, lg=lg: v.max_index(i8u[:], sc[:, 8:16], lg),
              reads=[sc[:, 8:16], lg], writes=[i8u], tag=tag + "mi")
        p.add("vector", lambda v: v.tensor_copy(sc[:, 1:2], i8u[:, 0:1]),
              reads=[i8u], writes=[sc[:, 1:2]], tag=tag + "if")
        p.add("vector", lambda v: v.tensor_scalar(sc[:, 24:32], C.iota8, sc[:, 1:2],
                                                  None, ALU.is_equal),
              reads=[C.iota8, sc[:, 1:2]], writes=[sc[:, 24:32]], tag=tag + "oh")
        p.add("vector", lambda v: v.scalar_tensor_tensor(sc[:, 32:40], sc[:, 24:32],
                                                         -1e30, sc[:, 32:40],
                                                         ALU.mult, ALU.add),
              reads=[sc[:, 24:32], sc[:, 32:40]], writes=[sc[:, 32:40]], tag=tag + "mk")
        wcw = cw_tm[:, tt*8:(tt+1)*8]
        _route_top2_common(C, sc[:, 32:40], sc, i8u, i8f, wcw, tt, "pair", tag + "t2")
        p.add("tensor", lambda t, wcw=wcw: t.transpose(ps_t32[0:8, 0:128], wcw,
                                                       C.ident_f32),
              reads=[wcw, C.ident_f32], writes=[ps_t32], tag=tag + "tp")
        p.add("vector", lambda v, tt=tt: v.tensor_copy(C.wrf_stage[0:8, 0:128],
                                                       ps_t32[0:8, 0:128]),
              reads=[ps_t32], writes=[C.wrf_stage], tag=tag + "we")
        p.dma("sync", "wrf", tt & 1, C.wrfm_dram[:, tt*128:(tt+1)*128],
              C.wrf_stage[0:8, 0:128], tag=tag + "wd")


def emit_route_l3_local(C, *, T, D, xn_chunk, rwT_sb, lg_tm, sc, i8u, prow, irow,
                        ps_a, ps_t32, tag=""):
    p = C.prog
    NTT = T // 128
    emit_logits_tm(C, T=T, D=D, xn_chunk=xn_chunk, wT_sb=rwT_sb, width=8,
                   out_tm=lg_tm, ps=ps_a, tag=tag + "lgt")
    for tt in range(NTT):
        lg = lg_tm[:, tt*8:(tt+1)*8]
        p.add("vector", lambda v, lg=lg: v.reduce_max(sc[:, 0:1], lg, AX.X),
              reads=[lg], writes=[sc[:, 0:1]], tag=tag + "rm")
        p.add("vector", lambda v: v.tensor_scalar_mul(sc[:, 1:2], sc[:, 0:1], -1.0),
              reads=[sc[:, 0:1]], writes=[sc[:, 1:2]], tag=tag + "ng")
        p.add("scalar", lambda s, lg=lg: s.activation(sc[:, 32:40], lg, AF.Exp,
                                                      bias=sc[:, 1:2],
                                                      accum_out=sc[:, 3:4]),
              reads=[lg, sc[:, 1:2]], writes=[sc[:, 32:40], sc[:, 3:4]], tag=tag + "ex")
        p.add("vector", lambda v: v.reciprocal(sc[:, 4:5], sc[:, 3:4]),
              reads=[sc[:, 3:4]], writes=[sc[:, 4:5]], tag=tag + "rc")
        p.add("vector", lambda v: v.max(sc[:, 8:16], sc[:, 32:40]),
              reads=[sc[:, 32:40]], writes=[sc[:, 8:16]], tag=tag + "m1")
        p.add("vector", lambda v: v.max_index(i8u[:], sc[:, 8:16], sc[:, 32:40]),
              reads=[sc[:, 8:16], sc[:, 32:40]], writes=[i8u], tag=tag + "ai")
        p.add("vector", lambda v: v.tensor_copy(sc[:, 6:7], i8u[:, 0:1]),
              reads=[i8u], writes=[sc[:, 6:7]], tag=tag + "ic")
        p.add("tensor", lambda t: t.transpose(ps_t32[0:1, 0:128], sc[:, 4:5],
                                              C.ident_f32),
              reads=[sc[:, 4:5], C.ident_f32], writes=[ps_t32], tag=tag + "t1")
        p.add("vector", lambda v, tt=tt: v.tensor_copy(prow[0:1, tt*128:(tt+1)*128],
                                                       ps_t32[0:1, 0:128]),
              reads=[ps_t32], writes=[prow[0:1, tt*128:(tt+1)*128]], tag=tag + "c1")
        p.add("tensor", lambda t: t.transpose(ps_t32[0:1, 0:128], sc[:, 6:7],
                                              C.ident_f32),
              reads=[sc[:, 6:7], C.ident_f32], writes=[ps_t32], tag=tag + "t2")
        p.add("vector", lambda v, tt=tt: v.tensor_copy(irow[0:1, tt*128:(tt+1)*128],
                                                       ps_t32[0:1, 0:128]),
              reads=[ps_t32], writes=[irow[0:1, tt*128:(tt+1)*128]], tag=tag + "c2")


def emit_route_l3_global(C, *, T, capacity, pe_all_dram, sa, sb_, sc_,
                         selmask_sb, rowsel_sb, sel8to64_sb, sel64to8_sb, econst_sb,
                         lg_tm, sc, i8u, cw_tm, wrfm, bis, ps_a, ps_b, ps_t32,
                         tag=""):
    p = C.prog
    NTT = T // 128
    prep, erep, pm = sa, sb_, sc_
    for c in range(8):
        for e in range(8):
            p.dma("sync", "l3rep", 0, prep[e*8 + c: e*8 + c + 1, 0:T], pe_all_dram[2*c],
                  tag=tag + "rp")
            p.dma("sync", "l3rep", 0, erep[e*8 + c: e*8 + c + 1, 0:T],
                  pe_all_dram[2*c + 1], tag=tag + "re")
    p.add("vector", lambda v: v.tensor_scalar(pm[:], erep[:], econst_sb[:, 0:1], None,
                                              ALU.is_equal),
          reads=[erep, econst_sb], writes=[pm], tag=tag + "eq")
    p.add("vector", lambda v: v.tensor_scalar_add(prep[:], prep[:], 1.0),
          reads=[prep], writes=[prep], tag=tag + "p1")
    p.add("vector", lambda v: v.tensor_tensor(pm[:], pm[:], prep[:], ALU.mult),
          reads=[pm, prep], writes=[pm], tag=tag + "p2")
    p.add("vector", lambda v: v.tensor_scalar_add(pm[:], pm[:], -1.0),
          reads=[pm], writes=[pm], tag=tag + "p3")
    gts = sa
    p.add("gpsimd", lambda g: g.memset(bis[0:8, 0:1], 0.0), writes=[bis[0:8, 0:1]],
          tag=tag + "b0")
    p.add("gpsimd", lambda g: g.memset(bis[0:8, 1:2], 1.0), writes=[bis[0:8, 1:2]],
          tag=tag + "b1")
    for it in range(38):
        p.add("vector", lambda v: v.tensor_tensor(bis[0:8, 2:3], bis[0:8, 0:1],
                                                  bis[0:8, 1:2], ALU.add),
              reads=[bis[0:8, 0:2]], writes=[bis[0:8, 2:3]], tag=tag + "m1")
        p.add("vector", lambda v: v.tensor_scalar_mul(bis[0:8, 2:3], bis[0:8, 2:3], 0.5),
              reads=[bis[0:8, 2:3]], writes=[bis[0:8, 2:3]], tag=tag + "m2")
        p.add("tensor", lambda t: t.matmul(ps_b[0:64, 0:1], sel8to64_sb[0:8, 0:64],
                                           bis[0:8, 2:3], start=True, stop=True),
              reads=[sel8to64_sb, bis[0:8, 2:3]], writes=[ps_b], tag=tag + "mb")
        p.add("vector", lambda v: v.tensor_copy(bis[0:64, 8:9], ps_b[0:64, 0:1]),
              reads=[ps_b], writes=[bis[0:64, 8:9]], tag=tag + "mc")
        p.add("vector", lambda v: v.tensor_scalar(gts[:], pm[:], bis[0:64, 8:9],
                                                  None, ALU.is_gt),
              reads=[pm, bis[0:64, 8:9]], writes=[gts], tag=tag + "gt")
        p.add("vector", lambda v: v.tensor_reduce(bis[0:64, 9:10], gts[:], AX.X,
                                                  ALU.add),
              reads=[gts], writes=[bis[0:64, 9:10]], tag=tag + "rs")
        p.add("tensor", lambda t: t.matmul(ps_b[0:8, 0:1], sel64to8_sb[0:64, 0:8],
                                           bis[0:64, 9:10], start=True, stop=True),
              reads=[sel64to8_sb, bis[0:64, 9:10]], writes=[ps_b], tag=tag + "c8")
        p.add("vector", lambda v: v.tensor_scalar(bis[0:8, 3:4], ps_b[0:8, 0:1],
                                                  float(capacity), None, ALU.is_ge),
              reads=[ps_b], writes=[bis[0:8, 3:4]], tag=tag + "cd")
        p.add("vector", lambda v: v.tensor_tensor(bis[0:8, 4:5], bis[0:8, 2:3],
                                                  bis[0:8, 0:1], ALU.subtract),
              reads=[bis[0:8, 2:3], bis[0:8, 0:1]], writes=[bis[0:8, 4:5]],
              tag=tag + "d1")
        p.add("vector", lambda v: v.scalar_tensor_tensor(bis[0:8, 0:1], bis[0:8, 4:5],
                                                         bis[0:8, 3:4], bis[0:8, 0:1],
                                                         ALU.mult, ALU.add),
              reads=[bis[0:8, 4:5], bis[0:8, 3:4], bis[0:8, 0:1]],
              writes=[bis[0:8, 0:1]], tag=tag + "lu")
        p.add("vector", lambda v: v.tensor_scalar(bis[0:8, 5:6], bis[0:8, 3:4], -1.0,
                                                  1.0, ALU.mult, ALU.add),
              reads=[bis[0:8, 3:4]], writes=[bis[0:8, 5:6]], tag=tag + "nc")
        p.add("vector", lambda v: v.tensor_tensor(bis[0:8, 6:7], bis[0:8, 2:3],
                                                  bis[0:8, 1:2], ALU.subtract),
              reads=[bis[0:8, 2:3], bis[0:8, 1:2]], writes=[bis[0:8, 6:7]],
              tag=tag + "d2")
        p.add("vector", lambda v: v.scalar_tensor_tensor(bis[0:8, 1:2], bis[0:8, 6:7],
                                                         bis[0:8, 5:6], bis[0:8, 1:2],
                                                         ALU.mult, ALU.add),
              reads=[bis[0:8, 6:7], bis[0:8, 5:6], bis[0:8, 1:2]],
              writes=[bis[0:8, 1:2]], tag=tag + "hu")
    for j, col in ((0, 10), (1, 11)):
        p.add("tensor", lambda t, j=j: t.matmul(ps_b[0:64, 0:1], sel8to64_sb[0:8, 0:64],
                                                bis[0:8, j:j+1], start=True, stop=True),
              reads=[sel8to64_sb, bis[0:8, j:j+1]], writes=[ps_b], tag=tag + "lb")
        p.add("vector", lambda v, col=col: v.tensor_copy(bis[0:64, col:col+1],
                                                         ps_b[0:64, 0:1]),
              reads=[ps_b], writes=[bis[0:64, col:col+1]], tag=tag + "lc")
    ge, gtv, eq = sb_, sa, sc_
    p.add("vector", lambda v: v.tensor_scalar(ge[:], pm[:], bis[0:64, 10:11], None,
                                              ALU.is_gt),
          reads=[pm, bis[0:64, 10:11]], writes=[ge], tag=tag + "ge")
    p.add("vector", lambda v: v.tensor_scalar(gtv[:], pm[:], bis[0:64, 11:12], None,
                                              ALU.is_gt),
          reads=[pm, bis[0:64, 11:12]], writes=[gtv], tag=tag + "gv")
    p.add("vector", lambda v: v.tensor_tensor(eq[:], ge[:], gtv[:], ALU.subtract),
          reads=[ge, gtv], writes=[eq], tag=tag + "eqv")
    p.add("vector", lambda v: v.tensor_reduce(bis[0:64, 9:10], gtv[:], AX.X, ALU.add),
          reads=[gtv], writes=[bis[0:64, 9:10]], tag=tag + "gc")
    p.add("tensor", lambda t: t.matmul(ps_b[0:8, 0:1], sel64to8_sb[0:64, 0:8],
                                       bis[0:64, 9:10], start=True, stop=True),
          reads=[sel64to8_sb, bis[0:64, 9:10]], writes=[ps_b], tag=tag + "gc8")
    p.add("vector", lambda v: v.tensor_scalar(bis[0:8, 12:13], ps_b[0:8, 0:1], -1.0,
                                              float(capacity), ALU.mult, ALU.add),
          reads=[ps_b], writes=[bis[0:8, 12:13]], tag=tag + "qt")
    p.add("tensor", lambda t: t.matmul(ps_b[0:64, 0:1], sel8to64_sb[0:8, 0:64],
                                       bis[0:8, 12:13], start=True, stop=True),
          reads=[sel8to64_sb, bis[0:8, 12:13]], writes=[ps_b], tag=tag + "qb")
    p.add("vector", lambda v: v.tensor_copy(bis[0:64, 13:14], ps_b[0:64, 0:1]),
          reads=[ps_b], writes=[bis[0:64, 13:14]], tag=tag + "qc")
    p.add("vector", lambda v: v.tensor_reduce(bis[0:64, 14:15], eq[:], AX.X, ALU.add),
          reads=[eq], writes=[bis[0:64, 14:15]], tag=tag + "et")
    p.add("tensor", lambda t: t.matmul(ps_b[0:8, 0:1], selmask_sb[0:64, 0:8],
                                       bis[0:64, 14:15], start=True, stop=True),
          reads=[selmask_sb, bis[0:64, 14:15]], writes=[ps_b], tag=tag + "of")
    p.add("vector", lambda v: v.tensor_copy(bis[0:8, 16:17], ps_b[0:8, 0:1]),
          reads=[ps_b], writes=[bis[0:8, 16:17]], tag=tag + "os")
    p.add("tensor", lambda t: t.matmul(ps_a[0:64, 0:1], sel8to64_sb[0:8, 0:64],
                                       bis[0:8, 16:17], start=True, stop=True),
          reads=[sel8to64_sb, bis[0:8, 16:17]], writes=[ps_a], tag=tag + "ob")
    p.add("vector", lambda v: v.tensor_copy(bis[0:64, 15:16], ps_a[0:64, 0:1]),
          reads=[ps_a], writes=[bis[0:64, 15:16]], tag=tag + "oc")
    scn = ge
    p.add("vector", lambda v: v.tensor_tensor_scan(scn[:], eq[:], eq[:],
                                                   0.0, ALU.add, ALU.bypass),
          reads=[eq], writes=[scn], tag=tag + "sc")
    p.add("vector", lambda v: v.tensor_tensor(scn[:], scn[:], eq[:], ALU.subtract),
          reads=[scn, eq], writes=[scn], tag=tag + "r1")
    p.add("vector", lambda v: v.tensor_scalar(scn[:], scn[:], bis[0:64, 15:16], None,
                                              ALU.add),
          reads=[scn, bis[0:64, 15:16]], writes=[scn], tag=tag + "r2")
    p.add("vector", lambda v: v.tensor_scalar(scn[:], scn[:], bis[0:64, 13:14], None,
                                              ALU.is_lt),
          reads=[scn, bis[0:64, 13:14]], writes=[scn], tag=tag + "r3")
    p.add("vector", lambda v: v.tensor_tensor(scn[:], scn[:], eq[:], ALU.mult),
          reads=[scn, eq], writes=[scn], tag=tag + "r4")
    p.add("vector", lambda v: v.tensor_tensor(scn[:], scn[:], gtv[:], ALU.add),
          reads=[scn, gtv], writes=[scn], tag=tag + "r5")
    keep_row = C.misc["l3keep"]
    for tcc in range(T // 512):
        p.add("tensor", lambda t, tcc=tcc: t.matmul(ps_b[0:1, 0:512],
                                                    rowsel_sb[0:64, 0:1],
                                                    scn[:, tcc*512:(tcc+1)*512],
                                                    start=True, stop=True),
              reads=[rowsel_sb, scn], writes=[ps_b], tag=tag + "kp")
        p.add("vector", lambda v, tcc=tcc: v.tensor_copy(
                  keep_row[0:1, tcc*512:(tcc+1)*512], ps_b[0:1, 0:512]),
              reads=[ps_b], writes=[keep_row[0:1, tcc*512:(tcc+1)*512]],
              tag=tag + "kc")
    idx_row = C.misc["l3idx"]
    for tt in range(NTT):
        p.add("tensor", lambda t, tt=tt: t.transpose(
                  ps_t32[:, 0:1], keep_row[0:1, tt*128:(tt+1)*128],
                  C.ident_f32[0:1, 0:1]),
              reads=[keep_row[0:1, tt*128:(tt+1)*128], C.ident_f32], writes=[ps_t32],
              tag=tag + "tk")
        p.add("vector", lambda v: v.tensor_copy(sc[:, 0:1], ps_t32[:, 0:1]),
              reads=[ps_t32], writes=[sc[:, 0:1]], tag=tag + "ck")
        p.add("tensor", lambda t, tt=tt: t.transpose(
                  ps_t32[:, 0:1], idx_row[0:1, tt*128:(tt+1)*128],
                  C.ident_f32[64:65, 64:65]),
              reads=[idx_row[0:1, tt*128:(tt+1)*128], C.ident_f32], writes=[ps_t32],
              tag=tag + "ti")
        p.add("vector", lambda v: v.tensor_copy(sc[:, 1:2], ps_t32[:, 0:1]),
              reads=[ps_t32], writes=[sc[:, 1:2]], tag=tag + "ci")
        p.add("vector", lambda v: v.tensor_scalar(sc[:, 24:32], C.iota8, sc[:, 1:2],
                                                  None, ALU.is_equal),
              reads=[C.iota8, sc[:, 1:2]], writes=[sc[:, 24:32]], tag=tag + "o1")
        wcw = cw_tm[:, tt*8:(tt+1)*8]
        p.add("vector", lambda v, wcw=wcw: v.tensor_scalar(wcw, sc[:, 24:32],
                                                           sc[:, 0:1], None, ALU.mult),
              reads=[sc[:, 24:32], sc[:, 0:1]], writes=[wcw], tag=tag + "w1")
        p.add("vector", lambda v: v.tensor_scalar(sc[:, 2:3], sc[:, 0:1], -1.0, 1.0,
                                                  ALU.mult, ALU.add),
              reads=[sc[:, 0:1]], writes=[sc[:, 2:3]], tag=tag + "nk")
        p.add("vector", lambda v, wcw=wcw: v.scalar_tensor_tensor(
                  wcw, C.oh0_8, sc[:, 2:3], wcw, ALU.mult, ALU.add),
              reads=[C.oh0_8, sc[:, 2:3], wcw], writes=[wcw], tag=tag + "w0")
        p.add("tensor", lambda t, wcw=wcw: t.transpose(ps_t32[0:8, 0:128], wcw,
                                                       C.ident_f32),
              reads=[wcw, C.ident_f32], writes=[ps_t32], tag=tag + "tp")
        p.add("vector", lambda v, tt=tt: v.tensor_copy(C.wrf_stage[0:8, 0:128],
                                                       ps_t32[0:8, 0:128]),
              reads=[ps_t32], writes=[C.wrf_stage], tag=tag + "we")
        p.dma("sync", "wrf", tt & 1, C.wrfm_dram[:, tt*128:(tt+1)*128],
              C.wrf_stage[0:8, 0:128], tag=tag + "wd")


def emit_attn_l0(C, *, T, D, xres_chunk, xn_chunk,
                 wq_dram, wk_dram, wv_dram, wg_dram, wo_dram,
                 skw1_sb, skw2_sb, skb2_sb, svw1_sb, svw2_sb, svb2_sb, gnorm_row,
                 s1_sb, wk32, zscan, wkb, wvb, qf32, qb, kb, vtm, gb, wktm,
                 zk_dram, zv_dram, ofin_dram, ofl, wbuf, cosfm, sinfm,
                 ps_a, ps_b, ps_c, ps_d, tag=""):
    p = C.prog
    ND = D // 128
    NT = T // 512
    NTT = T // 128
    scale = 1.0 / 16.0
    TH2 = 1024
    for path, (w1s, w2s, b2s, wXb, z_dram) in enumerate(
            ((skw1_sb, skw2_sb, skb2_sb, wkb, zk_dram),
             (svw1_sb, svw2_sb, svb2_sb, wvb, zv_dram))):
        for tc in range(NT):
            for d in range(ND):
                p.add("tensor", lambda t, d=d, tc=tc, w1s=w1s: t.matmul(
                          ps_d[0:16, 0:512], w1s[:, d*16:(d+1)*16],
                          xn_chunk(d)[:, tc*512:(tc+1)*512],
                          start=(d == 0), stop=(d == ND-1)),
                      reads=[w1s[:, d*16:(d+1)*16], xn_chunk(d)[:, tc*512:(tc+1)*512]],
                      writes=[ps_d], tag=f"{tag}s1")
            p.add("scalar", lambda s, tc=tc: s.copy(s1_sb[0:16, tc*512:(tc+1)*512],
                                                    ps_d[0:16, 0:512]),
                  reads=[ps_d], writes=[s1_sb[0:16, tc*512:(tc+1)*512]],
                  tag=f"{tag}s1e")
        carry = C.misc["l0carry"]
        for mc in range(2):
            p.add("gpsimd", lambda g: g.memset(carry[:, 0:1], 0.0), writes=[carry],
                  tag=f"{tag}cz")
            for half in range(2):
                for tci in range(2):
                    tc = half * 2 + tci
                    p.add("tensor", lambda t, mc=mc, tc=tc, w2s=w2s: t.matmul(
                              ps_d[:, 0:512], w2s[0:16, mc*128:(mc+1)*128],
                              s1_sb[0:16, tc*512:(tc+1)*512], start=True, stop=True),
                          reads=[w2s[0:16, mc*128:(mc+1)*128],
                                 s1_sb[0:16, tc*512:(tc+1)*512]],
                          writes=[ps_d], tag=f"{tag}s2")
                    w32s = wk32[:, tci*512:(tci+1)*512]
                    p.add("vector", lambda v, b2s=b2s, mc=mc, w32s=w32s: v.tensor_scalar(
                              w32s, ps_d[:, 0:512], b2s[:, mc:mc+1], None, ALU.add),
                          reads=[ps_d, b2s], writes=[w32s], tag=f"{tag}s2e")
                    p.add("vector", lambda v, w32s=w32s: v.tensor_scalar(
                              w32s, w32s, 32.0, -32.0, ALU.min, ALU.max),
                          reads=[w32s], writes=[w32s], tag=f"{tag}cl")
                    p.add("scalar", lambda s, w32s=w32s: s.activation(
                              w32s, w32s, AF.Exp),
                          reads=[w32s], writes=[w32s], tag=f"{tag}ex")
                    p.add("vector", lambda v, mc=mc, tc=tc, wXb=wXb, w32s=w32s:
                          v.tensor_copy(wXb[:, mc*T + tc*512: mc*T + tc*512 + 512],
                                        w32s),
                          reads=[w32s],
                          writes=[wXb[:, mc*T + tc*512: mc*T + tc*512 + 512]],
                          tag=f"{tag}cb")
                p.add("vector", lambda v: v.tensor_tensor_scan(
                          zscan[:, 0:TH2], wk32[:, 0:TH2], wk32[:, 0:TH2],
                          carry[:, 0:1], ALU.add, ALU.bypass),
                      reads=[wk32[:, 0:TH2], carry], writes=[zscan], tag=f"{tag}zs")
                p.add("vector", lambda v: v.tensor_copy(carry[:, 0:1],
                                                        zscan[:, TH2-1:TH2]),
                      reads=[zscan[:, TH2-1:TH2]], writes=[carry], tag=f"{tag}cc")
                for tti in range(8):
                    tt = half * 8 + tti
                    p.add("tensor", lambda t, tti=tti: t.transpose(
                              C.ps_t32[:, 0:128], zscan[:, tti*128:(tti+1)*128],
                              C.ident_f32),
                          reads=[zscan[:, tti*128:(tti+1)*128], C.ident_f32],
                          writes=[C.ps_t32], tag=f"{tag}zt")
                    p.add("vector", lambda v: v.tensor_copy(C.tmp512[:, 0:128],
                                                            C.ps_t32[:, 0:128]),
                          reads=[C.ps_t32], writes=[C.tmp512], tag=f"{tag}ze")
                    p.dma("sync", "l0z", tt & 1, z_dram[tt][:, mc*128:(mc+1)*128],
                          C.tmp512[:, 0:128], tag=f"{tag}zd")
    p.add("tensor", lambda t: t.matmul(ps_d[:, 0:256], C.ones_k1, gnorm_row[0:1, 0:256],
                                       start=True, stop=True),
          reads=[C.ones_k1, gnorm_row], writes=[ps_d], tag=f"{tag}gn")
    gnbc = C.misc["l0gnbc"]
    p.add("vector", lambda v: v.tensor_copy(gnbc[:], ps_d[:, 0:256]),
          reads=[ps_d], writes=[gnbc], tag=f"{tag}gne")
    for h in range(4):
        for which, wd in ((0, wq_dram), (1, wk_dram), (2, wv_dram), (3, wg_dram)):
            pp = (h * 4 + which) & 1
            slot = wbuf[:, pp*2048:(pp+1)*2048]
            src = wd[:, h*256:(h+1)*256].rearrange("(c p) o -> p c o", p=128)
            p.dma("gpsimd", "l0aw", pp, slot.rearrange("p (c o) -> p c o", o=256),
                  src, tag=f"{tag}wd")
            dst = qb if which == 0 else (kb if which == 1 else None)
            for tc in range(NT):
                for oc in range(2):
                    bank = ps_d if oc == 0 else ps_c
                    for d in range(ND):
                        p.add("tensor", lambda t, d=d, tc=tc, oc=oc, slot=slot,
                              bank=bank: t.matmul(
                                  bank[:, 0:512],
                                  slot[:, d*256 + oc*128: d*256 + oc*128 + 128],
                                  xn_chunk(d)[:, tc*512:(tc+1)*512],
                                  start=(d == 0), stop=(d == ND-1)),
                              reads=[slot[:, d*256 + oc*128: d*256 + oc*128 + 128],
                                     xn_chunk(d)[:, tc*512:(tc+1)*512]],
                              writes=[bank], tag=f"{tag}pj")
                if which <= 1:
                    p.add("scalar", lambda s: s.copy(qf32[:, 0:512], ps_d[:, 0:512]),
                          reads=[ps_d], writes=[qf32[:, 0:512]], tag=f"{tag}pq0")
                    p.add("scalar", lambda s: s.copy(qf32[:, 512:1024], ps_c[:, 0:512]),
                          reads=[ps_c], writes=[qf32[:, 512:1024]], tag=f"{tag}pq1")
                    cs = cosfm[:, tc*512:(tc+1)*512]
                    sn = sinfm[:, tc*512:(tc+1)*512]
                    t1 = C.misc["l0r1"]
                    t2 = C.misc["l0r2"]
                    p.add("vector", lambda v, cs=cs: v.tensor_tensor(
                              t1[:, 0:512], qf32[:, 0:512], cs, ALU.mult),
                          reads=[qf32[:, 0:512], cs], writes=[t1], tag=f"{tag}r1")
                    p.add("vector", lambda v, sn=sn: v.tensor_tensor(
                              t2[:, 0:512], qf32[:, 512:1024], sn, ALU.mult),
                          reads=[qf32[:, 512:1024], sn], writes=[t2], tag=f"{tag}r2")
                    p.add("vector", lambda v, dst=dst, tc=tc: v.tensor_tensor(
                              dst[:, tc*512:(tc+1)*512], t1[:, 0:512], t2[:, 0:512],
                              ALU.subtract),
                          reads=[t1, t2], writes=[dst[:, tc*512:(tc+1)*512]],
                          tag=f"{tag}r3")
                    p.add("vector", lambda v, cs=cs: v.tensor_tensor(
                              t1[:, 0:512], qf32[:, 512:1024], cs, ALU.mult),
                          reads=[qf32[:, 512:1024], cs], writes=[t1], tag=f"{tag}r4")
                    p.add("vector", lambda v, sn=sn: v.tensor_tensor(
                              t2[:, 0:512], qf32[:, 0:512], sn, ALU.mult),
                          reads=[qf32[:, 0:512], sn], writes=[t2], tag=f"{tag}r5")
                    p.add("vector", lambda v, dst=dst, tc=tc: v.tensor_tensor(
                              dst[:, T + tc*512: T + (tc+1)*512], t1[:, 0:512],
                              t2[:, 0:512], ALU.add),
                          reads=[t1, t2], writes=[dst[:, T + tc*512: T + (tc+1)*512]],
                          tag=f"{tag}r6")
                elif which == 2:
                    for oc in range(2):
                        bank = ps_d if oc == 0 else ps_c
                        p.add("scalar", lambda s, bank=bank: s.copy(
                                  C.tmp512b[:, 0:512], bank[:, 0:512]),
                              reads=[bank], writes=[C.tmp512b], tag=f"{tag}pv")
                        for i in range(4):
                            tt = tc*4 + i
                            p.add("tensor", lambda t, i=i: t.transpose(
                                      C.ps_t[:, 0:128], C.tmp512b[:, i*128:(i+1)*128],
                                      C.ident_bf),
                                  reads=[C.tmp512b, C.ident_bf], writes=[C.ps_t],
                                  tag=f"{tag}vt")
                            p.add("vector", lambda v, tt=tt, oc=oc: v.tensor_copy(
                                      vtm[:, tt*256 + oc*128: tt*256 + oc*128 + 128],
                                      C.ps_t[:, 0:128]),
                                  reads=[C.ps_t],
                                  writes=[vtm[:, tt*256 + oc*128: tt*256 + oc*128 + 128]],
                                  tag=f"{tag}ve")
                else:
                    for oc in range(2):
                        bank = ps_d if oc == 0 else ps_c
                        p.add("scalar", lambda s, oc=oc, tc=tc, bank=bank: s.copy(
                                  gb[:, oc*T + tc*512: oc*T + tc*512 + 512],
                                  bank[:, 0:512]),
                              reads=[bank],
                              writes=[gb[:, oc*T + tc*512: oc*T + tc*512 + 512]],
                              tag=f"{tag}pg")
        hb = (h % 2) * 64
        hc = h // 2
        for tt in range(NTT):
            p.add("tensor", lambda t, tt=tt, hb=hb, hc=hc: t.transpose(
                      C.ps_t[:, 0:64], wkb[hb:hb+64, hc*T + tt*128: hc*T + tt*128 + 128],
                      C.ident_bf[hb:hb+64, hb:hb+64]),
                  reads=[wkb[hb:hb+64, hc*T + tt*128: hc*T + tt*128 + 128], C.ident_bf],
                  writes=[C.ps_t], tag=f"{tag}wt")
            p.add("vector", lambda v, tt=tt: v.tensor_copy(wktm[:, tt*64:(tt+1)*64],
                                                           C.ps_t[:, 0:64]),
                  reads=[C.ps_t], writes=[wktm[:, tt*64:(tt+1)*64]], tag=f"{tag}we")
        for ti in range(NTT):
            zkt = C.misc["l0zk"]
            zvt = C.misc["l0zv"]
            p.dma("sync", "l0zl", ti & 1, C.misc["l0zld"][:, 0:64],
                  zk_dram[ti][:, h*64:(h+1)*64], tag=f"{tag}zlk")
            p.add("vector", lambda v: v.reciprocal(zkt[:, 0:64],
                                                   C.misc["l0zld"][:, 0:64]),
                  reads=[C.misc["l0zld"]], writes=[zkt], tag=f"{tag}zrk")
            p.add("vector", lambda v: v.tensor_scalar_mul(zkt[:, 0:64], zkt[:, 0:64],
                                                          scale),
                  reads=[zkt], writes=[zkt], tag=f"{tag}zsk")
            p.dma("sync", "l0zl", ti & 1, C.misc["l0zld"][:, 64:128],
                  zv_dram[ti][:, h*64:(h+1)*64], tag=f"{tag}zlv")
            p.add("vector", lambda v: v.reciprocal(zvt[:, 0:64],
                                                   C.misc["l0zld"][:, 64:128]),
                  reads=[C.misc["l0zld"]], writes=[zvt], tag=f"{tag}zrv")
            for si in range(ti + 1):
                for d in range(2):
                    p.add("tensor", lambda t, si=si, ti=ti, d=d: t.matmul(
                              ps_a[:, 0:128], kb[:, d*T + si*128: d*T + si*128 + 128],
                              qb[:, d*T + ti*128: d*T + ti*128 + 128],
                              start=(d == 0), stop=(d == 1)),
                          reads=[kb[:, d*T + si*128: d*T + si*128 + 128],
                                 qb[:, d*T + ti*128: d*T + ti*128 + 128]],
                          writes=[ps_a], tag=f"{tag}S")
                ssb = C.misc["l0ssb"]
                if si == ti:
                    p.add("vector", lambda v: v.tensor_tensor(ssb[:, 0:128],
                                                              ps_a[:, 0:128],
                                                              C.tri_f32a, ALU.mult),
                          reads=[ps_a, C.tri_f32a], writes=[ssb], tag=f"{tag}Sm")
                else:
                    p.add("scalar", lambda s: s.copy(ssb[:, 0:128], ps_a[:, 0:128]),
                          reads=[ps_a], writes=[ssb], tag=f"{tag}Se")
                p.add("tensor", lambda t, si=si, ti=ti: t.matmul(
                          ps_b[:, 0:64], ssb[:, 0:128], wktm[:, si*64:(si+1)*64],
                          start=(si == 0), stop=(si == ti)),
                      reads=[ssb, wktm[:, si*64:(si+1)*64]], writes=[ps_b],
                      tag=f"{tag}L")
            lt = C.misc["l0lt"]
            p.add("vector", lambda v: v.tensor_tensor(lt[:, 0:64], ps_b[:, 0:64],
                                                      zkt[:, 0:64], ALU.mult),
                  reads=[ps_b, zkt], writes=[lt], tag=f"{tag}lz")
            p.add("vector", lambda v: v.reduce_max(lt[:, 64:65], lt[:, 0:64], AX.X),
                  reads=[lt[:, 0:64]], writes=[lt[:, 64:65]], tag=f"{tag}lm")
            p.add("vector", lambda v: v.tensor_scalar_mul(lt[:, 65:66], lt[:, 64:65],
                                                          -1.0),
                  reads=[lt[:, 64:65]], writes=[lt[:, 65:66]], tag=f"{tag}ln")
            p.add("scalar", lambda s: s.activation(lt[:, 0:64], lt[:, 0:64], AF.Exp,
                                                   bias=lt[:, 65:66],
                                                   accum_out=lt[:, 66:67]),
                  reads=[lt[:, 0:64], lt[:, 65:66]], writes=[lt[:, 0:64], lt[:, 66:67]],
                  tag=f"{tag}le")
            p.add("vector", lambda v: v.reciprocal(lt[:, 67:68], lt[:, 66:67]),
                  reads=[lt[:, 66:67]], writes=[lt[:, 67:68]], tag=f"{tag}lr")
            p.add("vector", lambda v: v.tensor_scalar(lt[:, 0:64], lt[:, 0:64],
                                                      lt[:, 67:68], None, ALU.mult),
                  reads=[lt[:, 0:64], lt[:, 67:68]], writes=[lt[:, 0:64]],
                  tag=f"{tag}lp")
            pzb = C.misc["l0pz"]
            p.add("vector", lambda v: v.tensor_tensor(pzb[:, 0:64], lt[:, 0:64],
                                                      zvt[:, 0:64], ALU.mult),
                  reads=[lt[:, 0:64], zvt], writes=[pzb], tag=f"{tag}pz")
            p.add("tensor", lambda t, hb=hb: t.transpose(C.ps_t[hb:hb+64, 0:128],
                                                  pzb[:, 0:64], C.ident_bf,
                                                  tile_position=(0, hb)),
                  reads=[pzb, C.ident_bf], writes=[C.ps_t], tag=f"{tag}pt")
            pzT = C.misc["l0pzT"]
            p.add("vector", lambda v, hb=hb: v.tensor_copy(pzT[hb:hb+64, 0:128],
                                                    C.ps_t[hb:hb+64, 0:128]),
                  reads=[C.ps_t], writes=[pzT], tag=f"{tag}pe")
            for si in range(ti + 1):
                p.add("tensor", lambda t, si=si, hb=hb, hc=hc: t.matmul(
                          ps_a[:, 0:128], wvb[hb:hb+64, hc*T + si*128: hc*T + si*128 + 128],
                          pzT[hb:hb+64, 0:128], start=True, stop=True),
                      reads=[wvb[hb:hb+64, hc*T + si*128: hc*T + si*128 + 128], pzT],
                      writes=[ps_a], tag=f"{tag}A")
                asb = C.misc["l0ssb"]
                if si == ti:
                    p.add("vector", lambda v: v.tensor_tensor(asb[:, 0:128],
                                                              ps_a[:, 0:128],
                                                              C.tri_f32a, ALU.mult),
                          reads=[ps_a, C.tri_f32a], writes=[asb], tag=f"{tag}Am")
                else:
                    p.add("scalar", lambda s: s.copy(asb[:, 0:128], ps_a[:, 0:128]),
                          reads=[ps_a], writes=[asb], tag=f"{tag}Ae")
                p.add("tensor", lambda t, si=si, ti=ti: t.matmul(
                          ps_c[:, 0:256], asb[:, 0:128], vtm[:, si*256:(si+1)*256],
                          start=(si == 0), stop=(si == ti)),
                      reads=[asb, vtm[:, si*256:(si+1)*256]], writes=[ps_c],
                      tag=f"{tag}O")
            gt = C.misc["l0gt"]
            p.add("scalar", lambda s: s.activation(gt[:, 0:256], ps_c[:, 0:256],
                                                   AF.Square, accum_out=lt[:, 70:71]),
                  reads=[ps_c], writes=[gt, lt[:, 70:71]], tag=f"{tag}gs")
            p.add("vector", lambda v: v.tensor_scalar(lt[:, 71:72], lt[:, 70:71],
                                                      1.0 / 256.0, 1e-5,
                                                      ALU.mult, ALU.add),
                  reads=[lt[:, 70:71]], writes=[lt[:, 71:72]], tag=f"{tag}gq0")
            p.add("scalar", lambda s: s.activation(lt[:, 71:72], lt[:, 71:72], AF.Sqrt),
                  reads=[lt[:, 71:72]], writes=[lt[:, 71:72]], tag=f"{tag}gq")
            p.add("vector", lambda v: v.reciprocal(lt[:, 72:73], lt[:, 71:72]),
                  reads=[lt[:, 71:72]], writes=[lt[:, 72:73]], tag=f"{tag}gr")
            gtm = C.misc["l0gtm"]
            for oc in range(2):
                p.add("tensor", lambda t, oc=oc, ti=ti: t.transpose(
                          C.ps_t[:, 0:128], gb[:, oc*T + ti*128: oc*T + ti*128 + 128],
                          C.ident_bf),
                      reads=[gb[:, oc*T + ti*128: oc*T + ti*128 + 128], C.ident_bf],
                      writes=[C.ps_t], tag=f"{tag}gt")
                p.add("vector", lambda v, oc=oc: v.tensor_copy(
                          gtm[:, oc*128:(oc+1)*128], C.ps_t[:, 0:128]),
                      reads=[C.ps_t], writes=[gtm[:, oc*128:(oc+1)*128]],
                      tag=f"{tag}ge")
            sg = C.misc["l0sg"]
            p.add("scalar", lambda s: s.activation(sg[:, 0:256], gtm[:, 0:256],
                                                   AF.Sigmoid),
                  reads=[gtm], writes=[sg], tag=f"{tag}sg")
            p.add("vector", lambda v: v.tensor_scalar(gt[:, 0:256], ps_c[:, 0:256],
                                                      lt[:, 72:73], None, ALU.mult),
                  reads=[ps_c, lt[:, 72:73]], writes=[gt], tag=f"{tag}o1")
            p.add("vector", lambda v: v.tensor_tensor(gt[:, 0:256], gt[:, 0:256],
                                                      gnbc[:, 0:256], ALU.mult),
                  reads=[gt, gnbc], writes=[gt], tag=f"{tag}o2")
            p.add("vector", lambda v: v.tensor_tensor(gt[:, 0:256], gt[:, 0:256],
                                                      gtm[:, 0:256], ALU.mult),
                  reads=[gt, gtm], writes=[gt], tag=f"{tag}o3")
            ofin = C.misc["l0of"]
            p.add("vector", lambda v: v.tensor_tensor(ofin[:, 0:256], gt[:, 0:256],
                                                      sg[:, 0:256], ALU.mult),
                  reads=[gt, sg], writes=[ofin], tag=f"{tag}o4")
            for oc in range(2):
                p.add("tensor", lambda t, oc=oc: t.transpose(
                          C.ps_t[:, 0:128], ofin[:, oc*128:(oc+1)*128], C.ident_bf),
                      reads=[ofin[:, oc*128:(oc+1)*128], C.ident_bf], writes=[C.ps_t],
                      tag=f"{tag}ot")
                p.add("vector", lambda v, oc=oc: v.tensor_copy(
                          C.misc["l0ofT"][:, oc*128:(oc+1)*128], C.ps_t[:, 0:128]),
                      reads=[C.ps_t], writes=[C.misc["l0ofT"][:, oc*128:(oc+1)*128]],
                      tag=f"{tag}oe")
                p.dma("sync", "l0of", (ti + oc) & 1,
                      ofin_dram[h*2 + oc][:, ti*128:(ti+1)*128],
                      C.misc["l0ofT"][:, oc*128:(oc+1)*128], tag=f"{tag}od")
    for tc in range(NT):
        for cc in range(8):
            p.dma("gpsimd", "l0ol", cc & 1, ofl[:, cc*512:(cc+1)*512],
                  ofin_dram[cc][:, tc*512:(tc+1)*512], tag=f"{tag}ol")
        for o in range(ND):
            pp = o & 1
            slot = wbuf[:, pp*1024:pp*1024 + ND*128]
            src = wo_dram[:, o*128:(o+1)*128].rearrange("(c p) o -> p c o", p=128)
            p.dma("gpsimd", "l0wo", pp, slot.rearrange("p (c o) -> p c o", o=128), src,
                  tag=f"{tag}wo")
            bank = ps_a if (o & 1) == 0 else ps_b
            for cc in range(8):
                p.add("tensor", lambda t, cc=cc, slot=slot, bank=bank: t.matmul(
                          bank[:, 0:512], slot[:, cc*128:(cc+1)*128],
                          ofl[:, cc*512:(cc+1)*512], start=(cc == 0), stop=(cc == 7)),
                      reads=[slot[:, cc*128:(cc+1)*128], ofl[:, cc*512:(cc+1)*512]],
                      writes=[bank], tag=f"{tag}om")
            xs = xres_chunk(o)[:, tc*512:(tc+1)*512]
            p.add("vector", lambda v, xs=xs, bank=bank: v.tensor_tensor(
                      xs, xs, bank[:, 0:512], ALU.add),
                  reads=[xs, bank], writes=[xs], tag=f"{tag}oev")



T = 2048
D = 1024


def host_inputs(inputs, core_b):
    g = inputs
    m = {}
    m["x_in"] = np.ascontiguousarray(
        g["L0_attn_hidden_states"][core_b].T.reshape(8, 128, T))
    for nm in ("Wq", "Wk", "Wv", "Wg", "Wo"):
        m["l0a_" + nm] = np.ascontiguousarray(g["L0_attn_" + nm])
    m["l0a_skw1"] = np.ascontiguousarray(g["L0_attn_sk_w1"])
    m["l0a_skw2"] = np.ascontiguousarray(g["L0_attn_sk_w2"])
    m["l0a_skb2"] = np.ascontiguousarray(g["L0_attn_sk_b2"].reshape(2, 128).T)
    m["l0a_svw1"] = np.ascontiguousarray(g["L0_attn_sv_w1"])
    m["l0a_svw2"] = np.ascontiguousarray(g["L0_attn_sv_w2"])
    m["l0a_svb2"] = np.ascontiguousarray(g["L0_attn_sv_b2"].reshape(2, 128).T)
    m["l0a_gnorm"] = np.ascontiguousarray(g["L0_attn_gnorm_w"].reshape(1, 256))
    half = 128
    inv_freq = 1.0 / 10000.0 ** (np.arange(half, dtype=np.float64) / half)
    ang = np.arange(T, dtype=np.float64)[None, :] * inv_freq[:, None]
    m["l0a_cos"] = np.cos(ang).astype(np.float32)
    m["l0a_sin"] = np.sin(ang).astype(np.float32)
    m["l0f_gwT"] = np.ascontiguousarray(g["L0_ffn_gate_w"].T)
    m["l0f_wg"] = np.ascontiguousarray(g["L0_ffn_we_gate"])
    m["l0f_wu"] = np.ascontiguousarray(g["L0_ffn_we_up"])
    m["l0f_wd"] = np.ascontiguousarray(g["L0_ffn_we_down"])
    m["l0f_swg"] = np.ascontiguousarray(g["L0_ffn_sw_gate"])
    m["l0f_swu"] = np.ascontiguousarray(g["L0_ffn_sw_up"])
    m["l0f_swd"] = np.ascontiguousarray(g["L0_ffn_sw_down"])
    for nm in ("Wq", "Wk", "Wv"):
        m["l1a_" + nm] = np.ascontiguousarray(g["L1_attn_" + nm])
    b = np.zeros((128, 24), np.float32)
    for i, nm in enumerate(("bq", "bk", "bv")):
        b[:, i*8:(i+1)*8] = g["L1_attn_" + nm].reshape(8, 128).T
    m["l1a_b"] = b
    m["l1f_uzT"] = np.ascontiguousarray(g["L1_ffn_uzman_embeddings"].T)
    m["l1f_w1"] = np.ascontiguousarray(g["L1_ffn_W1"])
    m["l1f_w2"] = np.ascontiguousarray(g["L1_ffn_W2"])
    m["l1f_b1"] = np.ascontiguousarray(
        g["L1_ffn_b1"].reshape(8, 32, 128).transpose(2, 0, 1).reshape(128, 256))
    m["l1f_b2"] = np.ascontiguousarray(
        g["L1_ffn_b2"].reshape(8, 8, 128).transpose(2, 0, 1).reshape(128, 64))
    m["l1f_onb"] = np.ascontiguousarray(g["L1_ffn_onbellek_durumu"].reshape(1, 8))
    for nm in ("Wq", "Wk", "Wv", "Wo"):
        m["l2a_" + nm] = np.ascontiguousarray(g["L2_attn_" + nm])
    b = np.zeros((128, 48), np.float32)
    for i, nm in enumerate(("bq", "bk", "bv")):
        b[0:64, i*16:(i+1)*16] = g["L2_attn_" + nm].reshape(16, 64).T
    m["l2a_b"] = b
    m["l2a_bo"] = np.ascontiguousarray(g["L2_attn_bo"].reshape(8, 128).T)
    m["l2f_w"] = np.ascontiguousarray(g["L2_ffn_weight"])
    m["l2f_alpha"] = np.ascontiguousarray(g["L2_ffn_alpha"].reshape(1, 1))
    m["l2f_b"] = np.ascontiguousarray(g["L2_ffn_bias"].reshape(8, 128).T)
    m["l3a_wqkv"] = np.ascontiguousarray(g["L3_attn_w_qkv"])
    m["l3a_wout"] = np.ascontiguousarray(g["L3_attn_w_out"])
    m["l3a_bout"] = np.ascontiguousarray(g["L3_attn_b_out"].reshape(8, 128).T)
    m["l3f_rwT"] = np.ascontiguousarray(g["L3_ffn_router_w"].T)
    m["l3f_w1"] = np.ascontiguousarray(g["L3_ffn_W1"])
    m["l3f_w2"] = np.ascontiguousarray(g["L3_ffn_W2"])
    m["l3f_b1"] = np.ascontiguousarray(
        g["L3_ffn_b1"].reshape(8, 32, 128).transpose(2, 0, 1).reshape(128, 256))
    m["l3f_b2"] = np.ascontiguousarray(
        g["L3_ffn_b2"].reshape(8, 8, 128).transpose(2, 0, 1).reshape(128, 64))
    c0 = core_b
    selmask = np.zeros((64, 8), np.float32)
    rowsel = np.zeros((64, 1), np.float32)
    econst = np.zeros((64, 1), np.float32)
    s864 = np.zeros((8, 64), np.float32)
    s648 = np.zeros((64, 8), np.float32)
    for pp in range(64):
        e, c = pp // 8, pp % 8
        econst[pp, 0] = e
        s864[e, pp] = 1.0
        s648[pp, e] = 1.0
        if c < c0:
            selmask[pp, e] = 1.0
        if c == c0:
            rowsel[pp, 0] = 1.0
    m["l3_selmask"] = selmask
    m["l3_rowsel"] = rowsel
    m["l3_econst"] = econst
    m["l3_s864"] = s864
    m["l3_s648"] = s648
    return m


def build_program(layers, l3_mode=None):
    import contextlib
    nc = bass.Bass("TRN2", target_bir_lowering=False)
    F = mybir.dt.float32

    def din(name, shape, dt=F):
        return nc.dram_tensor(name, shape, dt, kind="ExternalInput").ap()

    x_in = din("x_in", [8, 128, T])
    A = {}
    A["wrfm"] = nc.dram_tensor("wrfm_scr", [8, T], F).ap()
    x_out = nc.dram_tensor("x_out", [8, 128, T], F, kind="ExternalOutput").ap()

    need = set(layers)
    if 0 in need:
        for nm in ("Wq", "Wk", "Wv", "Wg", "Wo"):
            A["l0a_" + nm] = din("l0a_" + nm, [D, D])
        A["l0a_skw1"] = din("l0a_skw1", [D, 16])
        A["l0a_skw2"] = din("l0a_skw2", [16, 256])
        A["l0a_skb2"] = din("l0a_skb2", [128, 2])
        A["l0a_svw1"] = din("l0a_svw1", [D, 16])
        A["l0a_svw2"] = din("l0a_svw2", [16, 256])
        A["l0a_svb2"] = din("l0a_svb2", [128, 2])
        A["l0a_gnorm"] = din("l0a_gnorm", [1, 256])
        A["l0a_cos"] = din("l0a_cos", [128, T])
        A["l0a_sin"] = din("l0a_sin", [128, T])
        A["l0f_gwT"] = din("l0f_gwT", [D, 8])
        A["l0f_wg"] = din("l0f_wg", [8, D, 2752])
        A["l0f_wu"] = din("l0f_wu", [8, D, 2752])
        A["l0f_wd"] = din("l0f_wd", [8, 2752, D])
        A["l0f_swg"] = din("l0f_swg", [D, 2752])
        A["l0f_swu"] = din("l0f_swu", [D, 2752])
        A["l0f_swd"] = din("l0f_swd", [2752, D])
        A["zk_dram"] = nc.dram_tensor("zk_scr", [16, 128, 256], F).ap()
        A["zv_dram"] = nc.dram_tensor("zv_scr", [16, 128, 256], F).ap()
        A["ofin_dram"] = nc.dram_tensor("ofin_scr", [8, 128, T], BF16).ap()
    if 1 in need:
        for nm in ("Wq", "Wk", "Wv"):
            A["l1a_" + nm] = din("l1a_" + nm, [D, D])
        A["l1a_b"] = din("l1a_b", [128, 24])
        A["l1f_uzT"] = din("l1f_uzT", [D, 8])
        A["l1f_w1"] = din("l1f_w1", [8, D, 4096])
        A["l1f_w2"] = din("l1f_w2", [8, 4096, D])
        A["l1f_b1"] = din("l1f_b1", [128, 256])
        A["l1f_b2"] = din("l1f_b2", [128, 64])
        A["l1f_onb"] = din("l1f_onb", [1, 8])
        A["l1nst"] = nc.dram_tensor("l1nst_scr", [32, 128, 512], F).ap()
        A["l1vtm"] = nc.dram_tensor("l1vtm_scr", [16, 128, 1024], BF16).ap()
    if 2 in need:
        for nm in ("Wq", "Wk", "Wv", "Wo"):
            A["l2a_" + nm] = din("l2a_" + nm, [D, D])
        A["l2a_b"] = din("l2a_b", [128, 48])
        A["l2a_bo"] = din("l2a_bo", [128, 8])
        A["l2f_w"] = din("l2f_w", [D, D])
        A["l2f_alpha"] = din("l2f_alpha", [1, 1])
        A["l2f_b"] = din("l2f_b", [128, 8])
    if 3 in need or l3_mode == "head":
        A["l3f_rwT"] = din("l3f_rwT", [D, 8])
    if 3 in need:
        A["l3a_wqkv"] = din("l3a_wqkv", [D, 3 * D])
        A["l3a_wout"] = din("l3a_wout", [D, D])
        A["l3a_bout"] = din("l3a_bout", [128, 8])
        A["l3f_w1"] = din("l3f_w1", [8, D, 4096])
        A["l3f_w2"] = din("l3f_w2", [8, 4096, D])
        A["l3f_b1"] = din("l3f_b1", [128, 256])
        A["l3f_b2"] = din("l3f_b2", [128, 64])
        A["l3_selmask"] = din("l3_selmask", [64, 8])
        A["l3_rowsel"] = din("l3_rowsel", [64, 1])
        A["l3_econst"] = din("l3_econst", [64, 1])
        A["l3_s864"] = din("l3_s864", [8, 64])
        A["l3_s648"] = din("l3_s648", [64, 8])
        if l3_mode == "input":
            A["pe_all"] = din("pe_all", [16, T])
        else:
            A["pe_all"] = nc.dram_tensor("pe_all_scr", [16, T], F,
                                         addr_space="Shared").ap()
            A["pe_loc"] = nc.dram_tensor("pe_loc_scr", [2, T], F).ap()
    if l3_mode == "head":
        pe_out = nc.dram_tensor("pe_out", [2, T], F, kind="ExternalOutput").ap()

    es = contextlib.ExitStack()
    es.__enter__()
    sb = lambda n, s, d: es.enter_context(nc.sbuf_tensor(n, s, d))
    XRES = sb("XRES", [128, 8 * T], F)
    XN = sb("XN", [128, 8 * T], BF16)
    POOLB = sb("POOLB", [128, 22528], BF16)
    MED = sb("MED", [128, 4 * T], BF16)
    F32A = sb("F32A", [128, 4096], F)
    SMALLB = sb("SMALLB", [128, 5120], BF16)
    WSB = sb("WSB", [128, 4096], BF16)
    T512 = sb("T512", [128, 512], F)
    T512B = sb("T512B", [128, 512], BF16)
    SC = sb("SC", [128, 64], F)
    I8U = sb("I8U", [128, 8], U32)
    I8F = sb("I8F", [128, 8], F)
    LGTM = sb("LGTM", [128, 16 * 8], F)
    CWTM = sb("CWTM", [128, 16 * 8], F)
    WRFS = sb("WRFS", [8, 128], F)
    WB = sb("WB", [128, T], F)
    CONS = sb("CONS", [128, 132], F)
    CONSB = sb("CONSB", [128, 274], BF16)
    ONEK = sb("ONEK", [1, 160], F)
    MCOL = sb("MCOL", [128, 704], F)
    psum = [es.enter_context(nc.psum_tensor(f"PS{i}", [128, 512], F)) for i in range(6)]
    ps_t = es.enter_context(nc.psum_tensor("PST", [128, 1024], BF16))
    ps_t32 = es.enter_context(nc.psum_tensor("PST32", [128, 512], F))

    prog = Prog(nc, es)
    C = Ctx(nc, prog)
    C.ones_col = CONS[:, 0:1]
    C.ident_f32 = CONS[:, 4:132]
    C.tri_f32a = CONSB[:, 128:256]
    C.ident_bf = CONSB[:, 0:128]
    C.tri_bf = CONSB[:, 128:256]
    C.iota8 = CONSB[:, 256:264]
    C.ones_k1 = ONEK[0:1, 0:128]
    C.ones_col_bf = CONSB[:, 264:265]
    C.oh0_8 = CONSB[:, 266:274]
    C.ps_t = ps_t[:]
    C.ps_t32 = ps_t32[:]
    C.tmp512 = T512[:]
    C.tmp512b = T512B[:]
    C.tmp512c = F32A[:, 0:512]
    C.misc = {
        "l2red": MCOL[:, 0:1], "l2acc": MCOL[:, 1:2], "l2dcol": MCOL[:, 2:4],
        "l2acol": MCOL[:, 4:5], "l1negc": MCOL[:, 5:6], "l1rz": MCOL[:, 6:7],
    }
    C.wrf_stage = WRFS[0:8, 0:128]
    C.wrfm_dram = A["wrfm"]

    prog.add("gpsimd", lambda g: g.memset(CONS[:, 0:1], 1.0), writes=[CONS[:, 0:1]])
    prog.add("gpsimd", lambda g: g.memset(ONEK[0:1, 0:160], 1.0), writes=[ONEK[0:1, 0:160]])
    prog.add("gpsimd", lambda g: g.iota(CONSB[:, 256:264], pattern=[[1, 8]], base=0,
                                        channel_multiplier=0,
                                        allow_small_or_imprecise_dtypes=True),
             writes=[CONSB[:, 256:264]])
    prog.add("gpsimd", lambda g: g.iota(CONS[:, 1:2], pattern=[[0, 1]], base=0,
                                        channel_multiplier=1,
                                        allow_small_or_imprecise_dtypes=True),
             writes=[CONS[:, 1:2]])
    prog.add("gpsimd", lambda g: g.iota(T512[:, 0:128], pattern=[[1, 128]], base=0,
                                        channel_multiplier=0,
                                        allow_small_or_imprecise_dtypes=True),
             writes=[T512[:, 0:128]])
    prog.add("vector", lambda v: v.tensor_scalar(CONS[:, 4:132], T512[:, 0:128],
                                                 CONS[:, 1:2], None, ALU.is_equal),
             reads=[T512[:, 0:128], CONS[:, 1:2]], writes=[CONS[:, 4:132]])
    prog.add("vector", lambda v: v.tensor_scalar(CONSB[:, 128:256], T512[:, 0:128],
                                                 CONS[:, 1:2], None, ALU.is_ge),
             reads=[T512[:, 0:128], CONS[:, 1:2]], writes=[CONSB[:, 128:256]])
    prog.add("vector", lambda v: v.tensor_copy(CONSB[:, 0:128], CONS[:, 4:132]),
             reads=[CONS[:, 4:132]], writes=[CONSB[:, 0:128]])
    prog.add("gpsimd", lambda g: g.memset(CONSB[:, 264:265], 1.0),
             writes=[CONSB[:, 264:265]])
    prog.add("vector", lambda v: v.tensor_scalar(CONSB[:, 266:274], CONSB[:, 256:264],
                                                 0.0, None, ALU.is_equal),
             reads=[CONSB[:, 256:264]], writes=[CONSB[:, 266:274]])

    for d in range(8):
        prog.dma("sync", "xin", 0, XRES[:, d*T:(d+1)*T], x_in[d], tag="ldx", unique=True)

    xres_chunk = lambda o: XRES[:, o*T:(o+1)*T]
    xn_chunk = lambda d: XN[:, d*T:(d+1)*T]

    def do_layernorm(tg):
        emit_layernorm(C, T=T, D=D, xres_chunk=xres_chunk, xn_chunk=xn_chunk,
                       stats=F32A[0:1, 0:1536], ps_a=psum[0][:], ps_b=psum[1][:],
                       sq_sb=T512[:], tag=tg)

    for L in layers:
        do_layernorm(f"ln{L}.")
        if L == 0:
            l0gn = MCOL[:, 96:352]
            C.misc["l0gnbc"] = l0gn
            C.misc["l0carry"] = MCOL[:, 7:8]
            C.misc["l0r1"] = T512[:, 0:512]
            C.misc["l0r2"] = F32A[:, 3072:3584]
            C.misc["l0ssb"] = SMALLB[:, 2048:2176]
            C.misc["l0lt"] = MCOL[:, 352:432]
            C.misc["l0pz"] = SMALLB[:, 2176:2240]
            C.misc["l0pzT"] = SMALLB[:, 2240:2368]
            C.misc["l0gt"] = F32A[:, 3776:4032]
            C.misc["l0sg"] = MCOL[:, 448:704]
            C.misc["l0gtm"] = SMALLB[:, 2368:2624]
            C.misc["l0of"] = SMALLB[:, 2624:2880]
            C.misc["l0ofT"] = SMALLB[:, 2880:3136]
            C.misc["l0zld"] = F32A[:, 3584:3712]
            C.misc["l0zk"] = MCOL[:, 8:72]
            C.misc["l0zv"] = F32A[:, 3712:3776]
            skw1 = SMALLB[:, 3136:3264]
            skw2 = SMALLB[0:16, 3392:3648]
            svw1 = SMALLB[:, 3264:3392]
            svw2 = SMALLB[0:16, 3648:3904]
            skb2 = MCOL[:, 88:90]
            svb2 = MCOL[:, 90:92]
            gnr = F32A[0:1, 0:256]
            prog.dma("gpsimd", "l0c", 0, skw1.rearrange("p (c o) -> p c o", o=16),
                     A["l0a_skw1"].rearrange("(c p) o -> p c o", p=128), unique=True)
            prog.dma("gpsimd", "l0c", 0, skw2, A["l0a_skw2"], unique=True)
            prog.dma("gpsimd", "l0c", 0, svw1.rearrange("p (c o) -> p c o", o=16),
                     A["l0a_svw1"].rearrange("(c p) o -> p c o", p=128), unique=True)
            prog.dma("gpsimd", "l0c", 0, svw2, A["l0a_svw2"], unique=True)
            prog.dma("sync", "l0c2", 0, skb2, A["l0a_skb2"], unique=True)
            prog.dma("sync", "l0c2", 0, svb2, A["l0a_svb2"], unique=True)
            prog.dma("sync", "l0c2", 0, gnr, A["l0a_gnorm"], unique=True)
            prog.dma("gpsimd", "l0c", 0, POOLB[:, 16384:18432], A["l0a_cos"], unique=True)
            prog.dma("gpsimd", "l0c", 0, POOLB[:, 18432:20480], A["l0a_sin"], unique=True)
            emit_attn_l0(C, T=T, D=D, xres_chunk=xres_chunk, xn_chunk=xn_chunk,
                         wq_dram=A["l0a_Wq"], wk_dram=A["l0a_Wk"], wv_dram=A["l0a_Wv"],
                         wg_dram=A["l0a_Wg"], wo_dram=A["l0a_Wo"],
                         skw1_sb=skw1, skw2_sb=skw2, skb2_sb=skb2,
                         svw1_sb=svw1, svw2_sb=svw2, svb2_sb=svb2,
                         gnorm_row=gnr, s1_sb=SMALLB[0:16, 0:2048],
                         wk32=F32A[:, 1024:2048], zscan=F32A[:, 2048:3072],
                         wkb=POOLB[:, 0:2*T], wvb=MED[:, 0:2*T],
                         qf32=F32A[:, 0:1024], qb=POOLB[:, 2*T:4*T],
                         kb=POOLB[:, 4*T:6*T], vtm=MED[:, 2*T:4*T],
                         gb=POOLB[:, 6*T:8*T], wktm=SMALLB[:, 4096:5120],
                         zk_dram=A["zk_dram"], zv_dram=A["zv_dram"],
                         ofin_dram=A["ofin_dram"], ofl=MED[:, 0:8*512],
                         wbuf=WSB[:, 0:4096], cosfm=POOLB[:, 16384:18432],
                         sinfm=POOLB[:, 18432:20480],
                         ps_a=psum[0][:], ps_b=psum[1][:], ps_c=psum[2][:],
                         ps_d=psum[3][:], tag="a0.")
            gwT = SMALLB[:, 3904:3968]
            prog.dma("gpsimd", "l0c", 0, gwT.rearrange("p (c o) -> p c o", o=8),
                     A["l0f_gwT"].rearrange("(c p) o -> p c o", p=128), unique=True)
            emit_route_l0(C, T=T, D=D, xn_chunk=xn_chunk, gwT_sb=gwT,
                          lg_tm=LGTM[:], sc=SC[:], i8u=I8U[:], i8f=I8F[:],
                          cw_tm=CWTM[:], wrfm=None, ps_a=psum[4][:],
                          ps_t32=ps_t32[:], tag="r0.")
            emit_ffn_l0(C, T=T, D=D, xres_chunk=xres_chunk, xn_chunk=xn_chunk,
                        wg_dram=lambda e: A["l0f_wg"][e] if e < 8 else A["l0f_swg"],
                        wu_dram=lambda e: A["l0f_wu"][e] if e < 8 else A["l0f_swu"],
                        wd_dram=lambda e: A["l0f_wd"][e] if e < 8 else A["l0f_swd"],
                        wrow_src=lambda e: A["wrfm"][e:e+1, :],
                        wr_stage=F32A[0:1, 0:T],
                        h_sb=POOLB[:, 0:22528],
                        gu_buf=[WSB[:, 0:2048], WSB[:, 2048:4096]],
                        wd_buf=[MED[:, 0:2816], MED[:, 2816:5632]],
                        wb_sb=WB[:, 0:T], tmp_sb=T512[:],
                        ones_f32=C.ones_k1, ps_a=psum[0][:], ps_b=psum[1][:],
                        ps_c=psum[2][:], ps_d=psum[3][:], tag="f0.")
        elif L == 1:
            w_map = {0: A["l1a_Wq"], 1: A["l1a_Wk"], 2: A["l1a_Wv"]}
            emit_attn_l1(C, T=T, D=D, xres_chunk=xres_chunk, xn_chunk=xn_chunk,
                         w_dram=lambda i: w_map[i], b_sb=sbv(A, prog, "l1a_b", MCOL[:, 96:120]),
                         kfm=POOLB[:, 0:8*T], qfm=SMALLB[:, 0:4096],
                         et_sb=MED[:, 0:16*512], cells=F32A[0:1, 2048:2112],
                         qrow=F32A[0:1, 0:T], wbuf=WSB[:, 0:4096],
                         vst=[POOLB[:, 16384:18432], POOLB[:, 18432:22528]],
                         vtm_dram=A["l1vtm"], nst_dram=A["l1nst"],
                         ps_a=psum[0][:], ps_b=psum[1][:], ps_c=psum[2][:], tag="a1.")
            uzT = sb("UZT", [128, 8 * 8], BF16)
            prog.dma("gpsimd", "l1c", 0, uzT[:].rearrange("p (c o) -> p c o", o=8),
                     A["l1f_uzT"].rearrange("(c p) o -> p c o", p=128), unique=True)
            C.misc["l1bce"] = MCOL[:, 72:88]
            emit_route_l1(C, T=T, D=D, xn_chunk=xn_chunk, uzT_sb=uzT[:],
                          onb_dram=A["l1f_onb"], lg_tm=LGTM[:],
                          tnorm_row=F32A[0:1, 0:T], sc=SC[:], i8u=I8U[:],
                          i8f=I8F[:], cw_tm=CWTM[:], wrfm=None,
                          ps_a=psum[4][:], ps_b=psum[5][:], ps_t32=ps_t32[:],
                          tag="r1.")
            b1s = sbv(A, prog, "l1f_b1", MCOL[:, 120:376])
            b2s = sbv(A, prog, "l1f_b2", MCOL[:, 376:440])
            emit_ffn_expertsum(C, E=8, T=T, D=D, H=4096, HB=1024, act=AF.Gelu,
                               xn_chunk=xn_chunk, xres_chunk=xres_chunk,
                               w1_dram=lambda e: A["l1f_w1"][e],
                               w2_dram=lambda e: A["l1f_w2"][e],
                               b1_sb=b1s, b2_sb=b2s,
                               wrow_dram=lambda e: A["wrfm"][e:e+1, :],
                               wr_stage=F32A[0:1, 0:T],
                               h_sb=POOLB[:, 0:8*T],
                               w1buf=[POOLB[:, 16384:18432], POOLB[:, 18432:20480]],
                               w2buf=[WSB[:, 0:1024], WSB[:, 1024:2048]],
                               wb_sb=WB[:, 0:T], tmp_sb=T512[:],
                               ones_f32=C.ones_k1,
                               ps_up=[psum[0][:], psum[1][:]],
                               ps_y=[psum[2][:], psum[3][:]], ps_b=[psum[4][:]],
                               tag="f1.")
        elif L == 2:
            w_map = {0: A["l2a_Wq"], 1: A["l2a_Wk"], 2: A["l2a_Wv"]}
            emit_attn_l2(C, T=T, D=D, xres_chunk=xres_chunk, xn_chunk=xn_chunk,
                         w_dram=lambda i: w_map[i], b_sb=sbv(A, prog, "l2a_b", MCOL[:, 96:144]),
                         wo_dram=A["l2a_Wo"], bo_sb=sbv(A, prog, "l2a_bo", MCOL[:, 144:152]),
                         qh=SMALLB[0:64, 0:2048], kh=SMALLB[0:64, 2048:4096],
                         vtm=SMALLB[:, 4096:5120],
                         et_sb=MED[:, 0:16*512], zrow=F32A[0:1, 0:1024],
                         ctx=POOLB[:, 0:8*T], wbuf=WSB[:, 0:4096],
                         ps_a=psum[0][:], ps_b=psum[1][:], ps_c=psum[2][:], tag="a2.")
            C.misc["l2tern"] = SMALLB[:, 0:1024]
            C.misc["l2gt"] = F32A[:, 2048:3072]
            emit_ffn_l2(C, T=T, D=D, xres_chunk=xres_chunk, xn_chunk=xn_chunk,
                        w_dram=A["l2f_w"], alpha_dram=A["l2f_alpha"],
                        bias_sb=sbv(A, prog, "l2f_b", MCOL[:, 152:160]),
                        wstage=F32A[:, 0:2048], wtT=MED[:, 0:8192],
                        ps_a=psum[0][:], ps_b=psum[1][:],
                        sc1=F32A[0:1, 3072:3136], tag="f2.")
        elif L == 3:
            rwT = sb("RWT", [128, 8 * 8], BF16)
            prog.dma("gpsimd", "l3c", 0, rwT[:].rearrange("p (c o) -> p c o", o=8),
                     A["l3f_rwT"].rearrange("(c p) o -> p c o", p=128), unique=True)
            prow = F32A[0:1, 0:T]
            irow = F32A[0:1, T:2*T]
            emit_route_l3_local(C, T=T, D=D, xn_chunk=xn_chunk, rwT_sb=rwT[:],
                                lg_tm=LGTM[:], sc=SC[:], i8u=I8U[:], prow=prow,
                                irow=irow, ps_a=psum[4][:], ps_t32=ps_t32[:],
                                tag="r3l.")
            C.misc["l3idx"] = WB[64:65, 0:T]
            C.misc["l3keep"] = WB[0:1, 0:T]
            prog.dma("sync", "r3ix", 0, WB[64:65, 0:T], irow, tag="r3ix",
                     unique=True)
            if l3_mode == "collective":
                prog.dma("sync", "l3pe", 0, A["pe_loc"][0:1, :], prow, tag="peout")
                prog.dma("sync", "l3pe", 0, A["pe_loc"][1:2, :], irow, tag="peout2")
                core_list = list(range(8))
                prog.dma_like("gpsimd", "cc", 0,
                              lambda g: g.collective_compute(
                                  "AllGather", ALU.bypass,
                                  replica_groups=[core_list],
                                  ins=[A["pe_loc"]], outs=[A["pe_all"]]),
                              reads=[A["pe_loc"]], writes=[A["pe_all"]], tag="ag")
            selm = LGTM[0:64, 72:80]
            rows = LGTM[0:64, 80:81]
            econ = LGTM[0:64, 81:82]
            s864 = LGTM[0:8, 0:64]
            s648 = LGTM[0:64, 64:72]
            for nm, t_ in (("l3_selmask", selm), ("l3_rowsel", rows),
                           ("l3_econst", econ), ("l3_s864", s864),
                           ("l3_s648", s648)):
                prog.dma("sync", "l3cc", 0, t_, A[nm], unique=True)
            bis = SC[0:64, 40:60]
            emit_route_l3_global(C, T=T, capacity=2560, pe_all_dram=A["pe_all"],
                                 sa=F32A[0:64, 0:T], sb_=F32A[0:64, T:2*T],
                                 sc_=WB[0:64, 0:T],
                                 selmask_sb=selm, rowsel_sb=rows,
                                 sel8to64_sb=s864, sel64to8_sb=s648,
                                 econst_sb=econ, lg_tm=LGTM[:], sc=SC[:],
                                 i8u=I8U[:], cw_tm=CWTM[:], wrfm=None, bis=bis,
                                 ps_a=psum[4][:], ps_b=psum[5][:], ps_t32=ps_t32[:],
                                 tag="r3g.")
            l3kv = SMALLB[:, 2048:2120]
            if "a3" not in SKIP:
             emit_attn_l3(C, T=T, D=D, xres_chunk=xres_chunk, xn_chunk=xn_chunk,
                         wqkv_dram=A["l3a_wqkv"], wout_dram=A["l3a_wout"],
                         bout_sb=sbv(A, prog, "l3a_bout", MCOL[:, 96:104]),
                         qh=MED[0:64, 0:T], kh=MED[0:64, T:2*T],
                         ktm=MED[:, 2*T:2*T+16*72], vtm=MED[:, 2*T+16*72:2*T+32*72],
                         kv_sb=l3kv, zcol=None, ctx=POOLB[:, 0:8*T],
                         wbuf=WSB[:, 0:4096], den=F32A[0:1, 0:512],
                         ps_a=psum[0][:], ps_b=psum[1][:], ps_c=psum[2][:], tag="a3.")
            b1s = sbv(A, prog, "l3f_b1", MCOL[:, 120:376])
            b2s = sbv(A, prog, "l3f_b2", MCOL[:, 376:440])
            if "f3" not in SKIP:
             emit_ffn_expertsum(C, E=8, T=T, D=D, H=4096, HB=1024, act=AF.Relu,
                               xn_chunk=xn_chunk, xres_chunk=xres_chunk,
                               w1_dram=lambda e: A["l3f_w1"][e],
                               w2_dram=lambda e: A["l3f_w2"][e],
                               b1_sb=b1s, b2_sb=b2s,
                               wrow_dram=lambda e: A["wrfm"][e:e+1, :],
                               wr_stage=F32A[0:1, 0:T],
                               h_sb=POOLB[:, 0:8*T],
                               w1buf=[POOLB[:, 16384:18432], POOLB[:, 18432:20480]],
                               w2buf=[WSB[:, 0:1024], WSB[:, 1024:2048]],
                               wb_sb=WB[:, 0:T], tmp_sb=T512[:],
                               ones_f32=C.ones_k1,
                               ps_up=[psum[0][:], psum[1][:]],
                               ps_y=[psum[2][:], psum[3][:]], ps_b=[psum[4][:]],
                               tag="f3.")
    if l3_mode == "head":
        rwT = sb("RWT", [128, 8 * 8], BF16)
        prog.dma("gpsimd", "l3c", 0, rwT[:].rearrange("p (c o) -> p c o", o=8),
                 A["l3f_rwT"].rearrange("(c p) o -> p c o", p=128), unique=True)
        do_layernorm("ln3h.")
        emit_route_l3_local(C, T=T, D=D, xn_chunk=xn_chunk, rwT_sb=rwT[:],
                            lg_tm=LGTM[:], sc=SC[:], i8u=I8U[:],
                            prow=F32A[0:1, 0:T], irow=F32A[0:1, T:2*T],
                            ps_a=psum[4][:], ps_t32=ps_t32[:], tag="r3l.")
        prog.dma("sync", "peo", 0, pe_out[0:1, :], F32A[0:1, 0:T], tag="peout")
        prog.dma("sync", "peo", 0, pe_out[1:2, :], F32A[0:1, T:2*T], tag="peout2")

    for d in range(8):
        prog.dma("sync", "xout", 0, x_out[d], XRES[:, d*T:(d+1)*T], tag="stx")

    with nc.Block() as block:
        prog.finalize(block, final_waits=prog.final_dma_waits())
    return nc, prog, es


def sbv(A, prog, name, dst):
    if not hasattr(prog, "_sbv"):
        prog._sbv = {}
    if name in prog._sbv:
        return prog._sbv[name]
    prog.dma("sync", "sbv", 0, dst, A[name], tag="sbv" + name, unique=True)
    prog._sbv[name] = dst
    return dst



def _declared_inputs(nc):
    names = set()
    for alloc in nc.m.functions[0].allocations:
        if isinstance(alloc, mybir.MemoryLocationSet) and alloc.kind == "ExternalInput":
            names.add(alloc.memorylocations[0].name)
    return names


def kernel(**inputs):
    inputs = {k: np.asarray(v) for k, v in inputs.items()}
    maps = [host_inputs(inputs, b) for b in range(8)]

    ncA, progA, esA = build_program([0, 1, 2], l3_mode="head")
    decA = _declared_inputs(ncA)
    in_mapsA = [{k: v for k, v in m.items() if k in decA} for m in maps]
    resA = run_bass_kernel_spmd(ncA, in_mapsA, core_ids=list(range(8)))

    pe_all = np.zeros((16, T), np.float32)
    xs_mid = []
    for c in range(8):
        pe_all[2*c] = resA.results[c]["pe_out"][0]
        pe_all[2*c+1] = resA.results[c]["pe_out"][1]
        xs_mid.append(np.ascontiguousarray(resA.results[c]["x_out"]))

    ncB, progB, esB = build_program([3], l3_mode="input")
    decB = _declared_inputs(ncB)
    in_mapsB = []
    for c in range(8):
        m = {k: v for k, v in maps[c].items() if k in decB}
        m["x_in"] = xs_mid[c]
        m["pe_all"] = pe_all
        in_mapsB.append(m)
    resB = run_bass_kernel_spmd(ncB, in_mapsB, core_ids=list(range(8)))

    out = np.empty((8, 2048, 1024), np.float32)
    for c in range(8):
        out[c] = resB.results[c]["x_out"].reshape(1024, 2048).T
    return out
